# version 03
# baseline (speedup 1.0000x reference)
import numpy as np
from contextlib import ExitStack
import concourse.bass as bass
import concourse.mybir as mybir
from concourse.bass_utils import run_bass_kernel_spmd
from concourse.alu_op_type import AluOpType as OP

F32 = mybir.dt.float32
BF16 = mybir.dt.bfloat16
AF = mybir.ActivationFunctionType

D = 2048
NH = 32
P = 64
NS = 128
KC = 4
DFF = 8192
NCORES = 8
Q = 128
NPAIR = 16
NDT = 16

ENGINES = ("tensor", "vector", "scalar", "gpsimd", "sync")


class Prog:

    def __init__(self, nc, strict_dma=True):
        self.nc = nc
        self.strict_dma = strict_dma
        self.ops = []
        self.ctick = {e: 0 for e in ENGINES}
        self.dtick = {e: 0 for e in ENGINES}
        self.last_w = {}
        self.reads = {}
        self.fence_dep = None
        self.fence_applied = set()

    def _need(self, cur_eng, cur_is_dma, dep):
        kind, eng, tick = dep
        if (eng == cur_eng and kind == 'c' and not cur_is_dma
                and eng == 'tensor'):
            return None
        if kind == 'd' and self.strict_dma:
            return ('d', eng, self.dtick[eng])
        return dep

    def op(self, engine, fn, reads=(), writes=(), is_dma=False):
        deps = set()
        if self.fence_dep is not None and engine not in self.fence_applied:
            deps.add(self.fence_dep)
            self.fence_applied.add(engine)
        for r in reads:
            d = self.last_w.get(r)
            if d is not None:
                d2 = self._need(engine, is_dma, d)
                if d2:
                    deps.add(d2)
        for w in writes:
            d = self.last_w.get(w)
            if d is not None:
                d2 = self._need(engine, is_dma, d)
                if d2:
                    deps.add(d2)
            for kd, tick in self.reads.get(w, {}).items():
                d2 = self._need(engine, is_dma, (kd[0], kd[1], tick))
                if d2:
                    deps.add(d2)
        waits = {}
        for kind, eng, tick in deps:
            kk = (kind, eng)
            waits[kk] = max(waits.get(kk, 0), tick)
        if is_dma:
            self.dtick[engine] += 1
            me = ('d', engine, self.dtick[engine])
        else:
            self.ctick[engine] += 1
            me = ('c', engine, self.ctick[engine])
        self.ops.append((engine, fn, is_dma, waits))
        for w in writes:
            self.last_w[w] = me
            self.reads[w] = {}
        for r in reads:
            dd = self.reads.setdefault(r, {})
            kk = (me[0], me[1])
            dd[kk] = max(dd.get(kk, 0), me[2])
        return me

    def dma(self, queue, out, in_, reads=(), writes=(), **kw):
        return self.op(queue, lambda e: e.dma_start(out=out, in_=in_, **kw),
                       reads=reads, writes=writes, is_dma=True)

    def fence(self):
        waits = {}
        for e in ENGINES:
            if self.ctick[e] > 0:
                waits[('c', e)] = self.ctick[e]
            if self.dtick[e] > 0:
                waits[('d', e)] = self.dtick[e]
        self.ctick['gpsimd'] += 1
        me = ('c', 'gpsimd', self.ctick['gpsimd'])
        self.ops.append(('gpsimd', lambda e: e.nop(), False, waits))
        self.fence_dep = me
        self.fence_applied = {'gpsimd'}

    def finish(self):
        waits = {}
        for e in ENGINES:
            if self.ctick[e] > 0:
                waits[('c', e)] = self.ctick[e]
            if self.dtick[e] > 0:
                waits[('d', e)] = self.dtick[e]
        self.ops.append(('gpsimd', lambda e: e.nop(), False, waits))

    def emit(self):
        nc = self.nc
        with ExitStack() as stack:
            sems = {}
            for e in ENGINES:
                sems[('c', e)] = stack.enter_context(nc.semaphore(f"p_{e}"))
                sems[('d', e)] = stack.enter_context(nc.semaphore(f"pd_{e}"))
            block = stack.enter_context(nc.Block())
            plans = {e: [] for e in ENGINES}
            for engine, fn, is_dma, waits in self.ops:
                plans[engine].append((fn, is_dma, waits))

            def mk(engname):
                def body(eng):
                    for fn, is_dma, waits in plans[engname]:
                        for (kind, weng), tick in sorted(waits.items()):
                            eng.wait_ge(sems[(kind, weng)],
                                        tick * (16 if kind == 'd' else 1))
                        fn(eng).then_inc(
                            sems[('d' if is_dma else 'c', engname)],
                            16 if is_dma else 1)
                body.__name__ = f"body_{engname}"
                return body

            block.sync(mk('sync'))
            block.scalar(mk('scalar'))
            block.vector(mk('vector'))
            block.tensor(mk('tensor'))
            block.gpsimd(mk('gpsimd'))


def build_nc(M=2048, prefix_mode='flag', strict_dma=False, x2_input=False,
             do_ffn=True, debug_outputs=()):
    nc = bass.Bass("TRN2", target_bir_lowering=False, debug=False)
    MT = M // 512
    NCH = M // Q
    FW = M + 8

    def din(name, shape, dtype=F32):
        return nc.dram_tensor(name, shape, dtype, kind="ExternalInput").ap()

    def dout(name, shape, dtype=F32):
        return nc.dram_tensor(name, shape, dtype, kind="ExternalOutput").ap()

    io = {}
    if not x2_input:
        io['xmain'] = din("xmain", [D, M + 3])
        io['winp'] = din("winp", [NPAIR, 6, 128, NDT * 128])
        io['wdt'] = din("wdt", [128, NDT * 32])
        io['cwx'] = din("cwx", [128, NPAIR * KC])
        io['cwb'] = din("cwb", [128, NH * KC])
        io['cwc'] = din("cwc", [128, NH * KC])
        io['alog'] = din("alog", [128, 1])
        io['dtb'] = din("dtb", [128, 1])
        io['woutt'] = din("woutt", [16, 128, 2048])
        if prefix_mode == 'flag':
            io['xpref'] = din("xpref", [D, M + 3])
            io['flag'] = din("flag", [128, 1])
        elif prefix_mode == 'input':
            io['hinit'] = din("hinit", [128, NH * P])
    else:
        io['x2t'] = din("x2t", [D, M])
    if do_ffn:
        io['w1t'] = din("w1t", [64, 128, NDT * 128])
        io['b1c'] = din("b1c", [128, 64])
        io['w2t'] = din("w2t", [16, 128, 64 * 128])
        io['b2c'] = din("b2c", [128, 16])
    io['out_t'] = dout("out_t", [D, M])
    if 'hend' in debug_outputs:
        io['hend'] = dout("hend", [128, NH * P])
    if 'ymix' in debug_outputs:
        io['ymix'] = dout("ymix", [D, M], BF16)
    if 'dt' in debug_outputs:
        io['dbg_dtle'] = dout("dbg_dtle", [128, M])
    if 'yraw' in debug_outputs:
        io['dbg_yraw'] = dout("dbg_yraw", [128, M], BF16)
    if 'proj0' in debug_outputs:
        io['dbg_pxsb'] = dout("dbg_pxsb", [128, M + 8], BF16)
        io['dbg_pbb0'] = dout("dbg_pbb0", [128, M + 8], BF16)
    if 'conv0' in debug_outputs:
        io['dbg_xsb'] = dout("dbg_xsb", [128, M + 8], BF16)
        io['dbg_bb0'] = dout("dbg_bb0", [128, M + 8], BF16)
        io['dbg_cb0'] = dout("dbg_cb0", [128, M + 8], BF16)
        io['dbg_zsil'] = dout("dbg_zsil", [128, M], BF16)

    if do_ffn:
        w2q_d = nc.dram_tensor("w2q_d", [16, 128, 64 * 128], BF16).ap()
        hscr_d = nc.dram_tensor("hscr_d", [128, MT, 64, 512], BF16).ap()
    if not x2_input:
        woq_d = nc.dram_tensor("woq_d", [16, 128, 2048], BF16).ap()
        winp_d = nc.dram_tensor("winp_d", [NPAIR, 6, 128, NDT * 128], BF16).ap()
        wdt_d = nc.dram_tensor("wdt_d", [128, NDT * 32], BF16).ap()
        xm_d = nc.dram_tensor("xm_d", [D, M + 3], BF16).ap()
        if prefix_mode == 'flag':
            xp_d = nc.dram_tensor("xp_d", [D, M + 3], BF16).ap()
    x2_d = nc.dram_tensor("x2_d", [16, 128, M], BF16).ap()

    p = Prog(nc, strict_dma=strict_dma)
    g_stack = ExitStack()

    def mksb(stk):
        def sb(name, shape, dtype):
            return stk.enter_context(nc.sbuf_tensor(name, shape, dtype))
        return sb

    def mkps(stk):
        def psf(name, shape, dtype=F32):
            return stk.enter_context(nc.psum_tensor(name, shape, dtype))
        return psf

    sb = mksb(g_stack)
    psf = mkps(g_stack)

    ident_bf = sb("ident_bf", [128, 128], BF16)
    ident_f = sb("ident_f", [128, 128], F32)
    maskT = sb("maskT", [128, 128], F32)
    ones_row_f = sb("ones_row_f", [1, 128], F32)
    ones_col_b = sb("ones_col_b", [128, 1], BF16)
    ones_col_f = sb("ones_col_f", [128, 1], F32)
    blk2 = sb("blk2", [128, 2], BF16)
    blk2f = sb("blk2f", [128, 2], F32)
    bcast2f = sb("bcast2f", [2, 128], F32)
    eps5 = sb("eps5", [128, 1], F32)
    eps6 = sb("eps6", [128, 1], F32)
    psA = psf("psA", [128, 512])
    psB = psf("psB", [128, 512])
    psC = psf("psC", [128, 512])
    psD = psf("psD", [128, 512])

    g = 'gpsimd'
    p.op(g, lambda e: e.memset(ident_bf[:], 0.0), writes=['identb'])
    p.op(g, lambda e: e.affine_select(out=ident_bf[:], in_=ident_bf[:],
         compare_op=OP.not_equal, fill=1.0, base=0, pattern=[[-1, 128]],
         channel_multiplier=1), reads=['identb'], writes=['identb'])
    p.op(g, lambda e: e.memset(ident_f[:], 0.0), writes=['identf'])
    p.op(g, lambda e: e.affine_select(out=ident_f[:], in_=ident_f[:],
         compare_op=OP.not_equal, fill=1.0, base=0, pattern=[[-1, 128]],
         channel_multiplier=1), reads=['identf'], writes=['identf'])
    p.op(g, lambda e: e.memset(maskT[:], 0.0), writes=['maskT'])
    p.op(g, lambda e: e.affine_select(out=maskT[:], in_=maskT[:],
         compare_op=OP.is_ge, fill=-1e30, base=0, pattern=[[1, 128]],
         channel_multiplier=-1), reads=['maskT'], writes=['maskT'])
    p.op(g, lambda e: e.memset(ones_row_f[:], 1.0), writes=['onesrf'])
    p.op(g, lambda e: e.memset(eps5[:], 1e-5), writes=['eps5'])
    p.op(g, lambda e: e.memset(eps6[:], 1e-6), writes=['eps6'])
    p.op(g, lambda e: e.memset(ones_col_b[:], 1.0), writes=['onescb'])
    p.op(g, lambda e: e.memset(ones_col_f[:], 1.0), writes=['onescf'])
    p.op(g, lambda e: e.memset(blk2[:], 0.0), writes=['blk2'])
    p.op(g, lambda e: e.affine_select(out=blk2[:, 0:1], in_=blk2[:, 0:1],
         compare_op=OP.is_ge, fill=1.0, base=-64, pattern=[[0, 1]],
         channel_multiplier=1), reads=['blk2'], writes=['blk2'])
    p.op(g, lambda e: e.affine_select(out=blk2[:, 1:2], in_=blk2[:, 1:2],
         compare_op=OP.is_ge, fill=1.0, base=63, pattern=[[0, 1]],
         channel_multiplier=-1), reads=['blk2'], writes=['blk2'])
    p.op(g, lambda e: e.memset(blk2f[:], 0.0), writes=['blk2f'])
    p.op(g, lambda e: e.affine_select(out=blk2f[:, 0:1], in_=blk2f[:, 0:1],
         compare_op=OP.is_ge, fill=1.0, base=-64, pattern=[[0, 1]],
         channel_multiplier=1), reads=['blk2f'], writes=['blk2f'])
    p.op(g, lambda e: e.affine_select(out=blk2f[:, 1:2], in_=blk2f[:, 1:2],
         compare_op=OP.is_ge, fill=1.0, base=63, pattern=[[0, 1]],
         channel_multiplier=-1), reads=['blk2f'], writes=['blk2f'])
    p.op('tensor', lambda e: e.transpose(psA[0:2, 0:128], blk2f[:, 0:2],
         ident_f[:, :]), reads=['blk2f', 'identf'], writes=['psA'])
    p.op('vector', lambda e: e.tensor_copy(bcast2f[:, :], psA[0:2, 0:128]),
         reads=['psA'], writes=['bc2'])

    thr = {}
    if do_ffn:
        thr['t1'] = sb("thr1", [128, 1], F32)
        thr['n1'] = sb("nthr1", [128, 1], F32)
        thr['t2'] = sb("thr2", [128, 1], F32)
        thr['n2'] = sb("nthr2", [128, 1], F32)
    y_stack = ExitStack()
    if not x2_input:
        Yt = [y_stack.enter_context(nc.sbuf_tensor(f"Yt{i}", [128, M], BF16))
              for i in range(16)]
    m_stack = ExitStack()
    msb = mksb(m_stack)
    mps = mkps(m_stack)
    if do_ffn:
        sclbuf = msb("w2b", [128, 512], F32)
        sclpart = msb("sclpart", [128, 2], F32)
        sclrow = msb("sclrow", [1, 2], F32)
        w2mn = msb("w2mn", [128, 512], BF16)
        w2qs = msb("w2qs", [128, 512], BF16)

        def scale_pass(wt3d, nblk, blkw, tag, tkey, nkey, nelem):
            p.op('vector', lambda e: e.memset(sclpart[:, 0:1], 0.0),
                 writes=[f'sp_{tag}'])
            nsub = blkw // 512
            for i in range(nblk):
                for j in range(nsub):
                    p.dma('gpsimd', sclbuf[:, :], wt3d[i, :, j * 512:(j + 1) * 512],
                          reads=['sclb'], writes=['sclb'])
                    p.op('vector', lambda e: e.tensor_reduce(
                        out=sclpart[:, 1:2], in_=sclbuf[:, :],
                        axis=mybir.AxisListType.X, op=OP.add,
                        apply_absolute_value=True),
                        reads=['sclb'], writes=[f'sq_{tag}'])
                    p.op('vector', lambda e: e.tensor_tensor(
                        out=sclpart[:, 0:1], in0=sclpart[:, 0:1],
                        in1=sclpart[:, 1:2], op=OP.add),
                        reads=[f'sq_{tag}', f'sp_{tag}'], writes=[f'sp_{tag}'])
            p.op('tensor', lambda e: e.matmul(psA[0:1, 0:1], ones_col_f[:, 0:1],
                 sclpart[:, 0:1], start=True, stop=True),
                 reads=[f'sp_{tag}', 'onescf'], writes=['psA'])
            p.op('vector', lambda e: e.tensor_copy(sclrow[0:1, 0:1], psA[0:1, 0:1]),
                 reads=['psA'], writes=[f'srow_{tag}'])
            p.op('tensor', lambda e: e.matmul(psA[:, 1:2], ones_row_f[0:1, :],
                 sclrow[0:1, 0:1], start=True, stop=True),
                 reads=[f'srow_{tag}', 'onesrf'], writes=['psA'])
            p.op('vector', lambda e: e.tensor_scalar(
                out=thr[tkey][:, 0:1], in0=psA[:, 1:2], scalar1=0.7 / nelem,
                scalar2=None, op0=OP.mult),
                reads=['psA'], writes=[f'thr_{tkey}'])
            p.op('vector', lambda e: e.tensor_scalar(
                out=thr[nkey][:, 0:1], in0=psA[:, 1:2], scalar1=-0.7 / nelem,
                scalar2=None, op0=OP.mult),
                reads=['psA'], writes=[f'thr_{nkey}'])

    if not x2_input:
        if not do_ffn:
            sclbuf = msb("w2b", [128, 512], F32)
            w2qs = msb("w2qs", [128, 512], BF16)
        p.dma('gpsimd', sclbuf[:, 0:512], io['wdt'][:, :],
              reads=['sclb'], writes=['sclb'])
        p.op('gpsimd', lambda e: e.tensor_copy(w2qs[:, 0:512], sclbuf[:, 0:512]),
             reads=['sclb'], writes=['w2qs'])
        p.dma('gpsimd', wdt_d[:, :], w2qs[:, 0:512],
              reads=['w2qs'], writes=['wdtd'])

        def x_prepass(src, dst, tag):
            for ib in range(16):
                for q4 in range(5):
                    lo = q4 * 512
                    wdt_w = min(512, M + 3 - lo)
                    if wdt_w <= 0:
                        break
                    p.dma('gpsimd', sclbuf[:, 0:wdt_w],
                          src[ib * 128:(ib + 1) * 128, lo:lo + wdt_w],
                          reads=['sclb'], writes=['sclb'])
                    p.op('gpsimd', (lambda w_: lambda e: e.tensor_copy(
                        w2qs[:, 0:w_], sclbuf[:, 0:w_]))(wdt_w),
                        reads=['sclb'], writes=['w2qs'])
                    p.dma('gpsimd', dst[ib * 128:(ib + 1) * 128, lo:lo + wdt_w],
                          w2qs[:, 0:wdt_w],
                          reads=['w2qs'], writes=[f'{tag}_{ib}'])

        if prefix_mode == 'flag':
            x_prepass(io['xpref'], xp_d, 'xpd')
        for i in range(NPAIR):
            for t in range(6):
                for q4 in range(4):
                    sl = slice(q4 * 512, (q4 + 1) * 512)
                    p.dma('gpsimd', sclbuf[:, :], io['winp'][i, t, :, sl],
                          reads=['sclb'], writes=['sclb'])
                    p.op('gpsimd', lambda e: e.tensor_copy(w2qs[:, :], sclbuf[:, :]),
                         reads=['sclb'], writes=['w2qs'])
                    p.dma('gpsimd', winp_d[i, t, :, sl], w2qs[:, :],
                          reads=['w2qs'], writes=[f'winpd_{i}_{t}'])
        x_prepass(io['xmain'], xm_d, 'xmd')
        for dm in range(16):
            for half in range(4):
                sl = slice(half * 512, (half + 1) * 512)
                p.dma('gpsimd', sclbuf[:, :], io['woutt'][dm, :, sl],
                      reads=['sclb'], writes=['sclb'])
                p.op('gpsimd', lambda e: e.tensor_copy(w2qs[:, :], sclbuf[:, :]),
                     reads=['sclb'], writes=['w2qs'])
                p.dma('gpsimd', woq_d[dm, :, sl], w2qs[:, :],
                      reads=['w2qs'], writes=[f'woqd_{dm}'])

    if do_ffn:
        scale_pass(io['w1t'], 64, 2048, 'w1', 't1', 'n1', float(DFF * D))
        scale_pass(io['w2t'], 16, 8192, 'w2', 't2', 'n2', float(DFF * D))

        w2b = sclbuf
        for dm in range(16):
            for qr in range(16):
                sl = slice(qr * 512, (qr + 1) * 512)
                p.dma('gpsimd', w2b[:, :], io['w2t'][dm, :, sl],
                      reads=['sclb'], writes=['sclb'])
                p.op('vector', lambda e: e.tensor_scalar(
                    out=w2mn[:, :], in0=w2b[:, :], scalar1=thr['n2'][:, 0:1],
                    scalar2=None, op0=OP.is_le),
                    reads=['sclb', 'thr_n2'], writes=['w2mn'])
                p.op('vector', lambda e: e.scalar_tensor_tensor(
                    out=w2qs[:, :], in0=w2b[:, :], scalar=thr['t2'][:, 0:1],
                    in1=w2mn[:, :], op0=OP.is_ge, op1=OP.subtract),
                    reads=['sclb', 'thr_t2', 'w2mn'], writes=['w2qs'])
                p.dma('gpsimd', w2q_d[dm, :, sl], w2qs[:, :],
                      reads=['w2qs'], writes=[f'w2qd_{dm}'])

    if not x2_input:
        psE = mps("psE", [128, 512])
        psF = mps("psF", [128, 512])
        psG = mps("psG", [128, 512])
        psH = mps("psH", [128, 1024], BF16)

        XT = [msb(f"XT{i}", [128, FW], BF16) for i in range(16)]
        wpbq = msb("wpbq", [128, 2048], BF16)
        Acol = msb("Acol", [128, 1], F32)
        dtbc = msb("dtbc", [128, 1], F32)
        flagc = msb("flagc", [128, 1], F32)
        cwxr = msb("cwxr", [128, NPAIR * KC], F32)
        cwbr = msb("cwbr", [128, NH * KC], F32)
        cwcr = msb("cwcr", [128, NH * KC], F32)
        diag = [msb(f"diag{k}", [128, 128], BF16) for k in range(KC)]
        dtLE = msb("dtLE", [128, M], F32)
        dtsc = msb("dtsc", [32, 512], F32)
        dtsc2 = msb("dtsc2", [32, 512], F32)
        zeros32 = msb("zeros32", [32, 128], F32)
        dtlm = msb("dtlm", [128, NCH * 96], F32)
        H = msb("H", [128, NH * P], F32)
        alphab = msb("alphab", [128, 32], F32)
        xsb = msb("xsb", [128, FW], BF16)
        Bb = [msb("Bb0", [128, FW], BF16), msb("Bb1", [128, FW], BF16)]
        Cb = [msb("Cb0", [128, FW], BF16), msb("Cb1", [128, FW], BF16)]
        zsil = msb("zsil", [128, M], BF16)
        Lrow = [msb("Lrow0", [1, 128], F32), msb("Lrow1", [1, 128], F32)]
        eLrow = [msb("eLrow0", [1, 128], F32), msb("eLrow1", [1, 128], F32)]
        Dp = [msb("Dp0", [128, 128], F32), msb("Dp1", [128, 128], F32)]
        Mp = [msb("Mp0", [128, 128], F32), msb("Mp1", [128, 128], F32)]
        Gt = [msb("Gt0", [128, 128], BF16), msb("Gt1", [128, 128], BF16)]
        xctm = [msb("xctm0", [128, 128], BF16), msb("xctm1", [128, 128], BF16)]
        btm = [msb("btm0", [128, 128], BF16), msb("btm1", [128, 128], BF16)]
        dtx1 = [msb("dtx1_0", [128, 64], BF16), msb("dtx1_1", [128, 64], BF16)]
        dtx2 = [msb("dtx2_0", [128, 64], BF16), msb("dtx2_1", [128, 64], BF16)]
        eLtb = [msb("eLtb0", [128, 128], F32), msb("eLtb1", [128, 128], F32)]
        expLs = msb("expLs", [128, 32], F32)
        Hb2 = [msb("Hb2_0", [128, 128], BF16), msb("Hb2_1", [128, 128], BF16)]
        tg = msb("tg", [128, 512], BF16)

        p.op(g, lambda e: e.memset(zeros32[:], 0.0), writes=['z32'])
        p.op(g, lambda e: e.memset(H[:], 0.0),
             writes=[f'H_{i}' for i in range(NPAIR)])

        p.dma('sync', cwxr[:, :], io['cwx'][:, :], writes=['cwxr'])
        p.dma('sync', cwbr[:, :], io['cwb'][:, :], writes=['cwbr'])
        p.dma('sync', cwcr[:, :], io['cwc'][:, :], writes=['cwcr'])
        p.dma('sync', Acol[:, :], io['alog'][:, :], writes=['Acol'])
        p.dma('sync', dtbc[:, :], io['dtb'][:, :], writes=['dtbc'])
        p.op('scalar', lambda e: e.activation(out=Acol[:, 0:1], in_=Acol[:, 0:1],
             func=AF.Exp), reads=['Acol'], writes=['Acol'])
        p.op('vector', lambda e: e.tensor_scalar(out=Acol[:, 0:1], in0=Acol[:, 0:1],
             scalar1=-1.0, scalar2=None, op0=OP.mult),
             reads=['Acol'], writes=['Acol'])
        if prefix_mode == 'flag':
            p.dma('sync', flagc[:, :], io['flag'][:, :], writes=['flagc'])
        elif prefix_mode == 'input':
            p.dma('sync', H[:, :], io['hinit'][:, :],
                  reads=[f'H_{i}' for i in range(NPAIR)],
                  writes=[f'H_{i}' for i in range(NPAIR)])

        def load_x(dram_bf, tag):
            for i in range(16):
                p.dma('sync', XT[i][:, 0:M + 3], dram_bf[i * 128:(i + 1) * 128, :],
                      reads=[f'{tag}_{i}', f'XT{i}'], writes=[f'XT{i}'])

        def dt_path():
            p.dma('sync', wpbq[:, 0:NDT * 32], wdt_d[:, :],
                  reads=['wdtd', 'wpbq'], writes=['wpbq'])
            for tc in range(MT):
                for c in range(NDT):
                    p.op('tensor', (lambda cc, tt: lambda e: e.matmul(
                        psE[0:32, 0:512], wpbq[:, cc * 32:(cc + 1) * 32],
                        XT[cc][:, 3 + tt * 512: 3 + tt * 512 + 512],
                        start=(cc == 0), stop=(cc == NDT - 1)))(c, tc),
                        reads=['wpbq', f'XT{c}'], writes=['psE'])
                p.op('vector', lambda e: e.tensor_scalar(
                    out=dtsc[:, :], in0=psE[0:32, 0:512], scalar1=dtbc[0:32, 0:1],
                    scalar2=None, op0=OP.add), reads=['psE', 'dtbc'], writes=['dtsc'])
                p.op('vector', lambda e: e.tensor_scalar(
                    out=dtsc2[:, :], in0=dtsc[:, :], scalar1=0.0, scalar2=2.0,
                    op0=OP.max, op1=OP.mult), reads=['dtsc'], writes=['dtsc2'])
                p.op('vector', lambda e: e.scalar_tensor_tensor(
                    out=dtsc[:, :], in0=dtsc2[:, :], scalar=1.0, in1=dtsc[:, :],
                    op0=OP.mult, op1=OP.subtract), reads=['dtsc2', 'dtsc'],
                    writes=['dtsc'])
                p.op('scalar', lambda e: e.activation(out=dtsc[:, :], in_=dtsc[:, :],
                     func=AF.Exp, scale=-1.0), reads=['dtsc'], writes=['dtsc'])
                p.op('scalar', lambda e: e.activation(out=dtsc[:, :], in_=dtsc[:, :],
                     func=AF.Ln, bias=ones_col_f[0:32, 0:1]),
                     reads=['dtsc', 'onescf'], writes=['dtsc'])
                p.op('vector', (lambda tt: lambda e: e.scalar_tensor_tensor(
                    out=dtLE[0:32, tt * 512:(tt + 1) * 512], in0=dtsc2[:, :],
                    scalar=0.5, in1=dtsc[:, :], op0=OP.mult, op1=OP.add))(tc),
                    reads=['dtsc2', 'dtsc'], writes=['dtLE_dt'])

        def chunk_pre():
            for c in range(NCH):
                p.op('vector', (lambda cc: lambda e: e.tensor_scalar(
                    out=dtsc[:, 0:128], in0=dtLE[0:32, cc * 128:(cc + 1) * 128],
                    scalar1=Acol[0:32, 0:1], scalar2=None, op0=OP.mult))(c),
                    reads=['dtLE_dt', 'Acol'], writes=['dtsc'])
                p.op('vector', (lambda cc: lambda e: e.tensor_tensor_scan(
                    out=dtLE[32:64, cc * 128:(cc + 1) * 128], data0=dtsc[:, 0:128],
                    data1=zeros32[:, :], initial=0.0, op0=OP.add, op1=OP.add))(c),
                    reads=['dtsc', 'z32'], writes=['dtLE_L'])
                p.op('vector', (lambda cc: lambda e: e.tensor_scalar(
                    out=dtsc[:, 0:128], in0=dtLE[32:64, cc * 128:(cc + 1) * 128],
                    scalar1=dtLE[32:64, cc * 128 + 127:cc * 128 + 128],
                    scalar2=None, op0=OP.subtract))(c),
                    reads=['dtLE_L'], writes=['dtsc'])
                p.op('scalar', (lambda cc: lambda e: e.activation(
                    out=dtLE[64:96, cc * 128:(cc + 1) * 128], in_=dtsc[:, 0:128],
                    func=AF.Exp, scale=-1.0))(c), reads=['dtsc'], writes=['dtLE_ef'])
                p.op('tensor', (lambda cc: lambda e: e.transpose(
                    psG[:, 0:96], dtLE[0:96, cc * 128:(cc + 1) * 128],
                    ident_f[0:96, 0:96]))(c),
                    reads=['dtLE_dt', 'dtLE_L', 'dtLE_ef', 'identf'],
                    writes=['psG'])
                p.op('vector', (lambda cc: lambda e: e.tensor_copy(
                    dtlm[:, cc * 96:(cc + 1) * 96], psG[:, 0:96]))(c),
                    reads=['psG'], writes=[f'dtlm_{c}'])

        def proj_pair(i, tiles, halo):
            for t in tiles:
                p.dma('sync', wpbq[:, :], winp_d[i, t, :, :],
                      reads=[f'winpd_{i}_{t}', 'wpbq'], writes=['wpbq'])
                tcs = list(range(MT)) + ([-1] if (halo and t != 1) else [])
                for tc in tcs:
                    if tc >= 0:
                        lo, w = 3 + tc * 512, 512
                    else:
                        lo, w = 0, 3
                    pbank, pkey = (psA, 'psA') if (tc % 2 == 0) else (psB, 'psB')
                    for c in range(NDT):
                        p.op('tensor', (lambda cc, lo_, w_, pb: lambda e: e.matmul(
                            pb[:, 0:w_], wpbq[:, cc * 128:(cc + 1) * 128],
                            XT[cc][:, lo_:lo_ + w_],
                            start=(cc == 0), stop=(cc == NDT - 1)))(c, lo, w, pbank),
                            reads=['wpbq', f'XT{c}'], writes=[pkey])
                    if t == 0:
                        dst, dk, func = xsb, 'xsb', AF.Copy
                    elif t == 1:
                        dst, dk, func = zsil, 'zsil', AF.Silu
                    elif t in (2, 3):
                        dst, dk, func = Bb[t - 2], f'Bb{t - 2}', AF.Copy
                    else:
                        dst, dk, func = Cb[t - 4], f'Cb{t - 4}', AF.Copy
                    if t == 1:
                        dsl = slice(lo - 3, lo - 3 + w)
                        wkeys = [f'{dk}_t{tc}']
                    elif tc < 0:
                        dsl = slice(0, 3)
                        wkeys = [f'{dk}_h']
                    else:
                        dsl = slice(lo, lo + w)
                        wkeys = [f'{dk}_t{tc}']
                    p.op('scalar', (lambda d, s_, pb, fn, w_: lambda e: e.activation(
                        out=d[:, s_], in_=pb[:, 0:w_], func=fn))(dst, dsl, pbank,
                                                                 func, w),
                        reads=[pkey], writes=wkeys)

        def conv_pair(i, srcs):
            banks = [(psA, 'psA'), (psB, 'psB'), (psC, 'psC'), (psD, 'psD')]
            for (src, skey, wtile, wtkey, wcol) in srcs:
                for k in range(KC):
                    p.op('vector', (lambda kk, wt, wc: lambda e: e.tensor_scalar(
                        out=diag[kk][:, :], in0=ident_bf[:, :],
                        scalar1=wt[:, wc + kk:wc + kk + 1], scalar2=None,
                        op0=OP.mult))(k, wtile, wcol),
                        reads=['identb', wtkey], writes=[f'diag{k}'])
                for tc in range(MT):
                    pb, pk = banks[tc % 4]
                    rd = [f'{skey}_t{tc}']
                    rd.append(f'{skey}_h' if tc == 0 else f'{skey}_t{tc - 1}')
                    for k in range(KC):
                        p.op('tensor', (lambda kk, tt, pbb, s2: lambda e: e.matmul(
                            pbb[:, 0:512], diag[kk][:, :],
                            s2[:, tt * 512 + kk:tt * 512 + kk + 512],
                            start=(kk == 0), stop=(kk == KC - 1)))(k, tc, pb, src),
                            reads=[f'diag{k}'] + rd, writes=[pk])
                for tc in range(MT):
                    pb, pk = banks[tc % 4]
                    p.op('scalar', (lambda tt, pbb, s2: lambda e: e.activation(
                        out=s2[:, 3 + tt * 512:3 + tt * 512 + 512],
                        in_=pbb[:, 0:512], func=AF.Silu))(tc, pb, src),
                        reads=[pk], writes=[f'{skey}_t{tc}'])

        def ssd_pair(i, full):
            h0 = 2 * i
            for c in range(NCH):
                tb = (c * 128) // 512
                cp = c % 2
                p.op('scalar', (lambda cc: lambda e: e.activation(
                    out=expLs[:, :], in_=dtlm[:, cc * 96 + 32:cc * 96 + 64],
                    func=AF.Exp))(c), reads=[f'dtlm_{c}', 'expLs'], writes=['expLs'])
                p.op('vector', (lambda cc: lambda e: e.tensor_tensor(
                    out=alphab[:, :], in0=dtlm[:, cc * 96 + 64:cc * 96 + 96],
                    in1=expLs[:, :], op=OP.mult))(c),
                    reads=[f'dtlm_{c}', 'expLs', 'alphab'], writes=['alphab'])
                p.op('tensor', (lambda cc, kp: lambda e: e.transpose(
                    psH[:, 256 * kp:256 * kp + 128],
                    xsb[:, 3 + cc * 128:3 + cc * 128 + 128],
                    ident_bf[:, :]))(c, cp),
                    reads=[f'xsb_t{tb}', 'identb'], writes=['psH'])
                p.op('vector', (lambda kp: lambda e: e.tensor_copy(
                    xctm[kp][:, :], psH[:, 256 * kp:256 * kp + 128]))(cp),
                    reads=['psH', f'xctm{cp}'], writes=[f'xctm{cp}'])
                if full:
                    p.op('vector', (lambda ii, kp: lambda e: e.tensor_copy(
                        Hb2[kp][:, :], H[:, ii * 128:(ii + 1) * 128]))(i, cp),
                        reads=[f'H_{i}', f'Hb{cp}'], writes=[f'Hb{cp}'])
                for hl in range(2):
                    h = h0 + hl
                    bkey = f'Bb{hl}_t{tb}'
                    ckey = f'Cb{hl}_t{tb}'
                    spg, spgk = (psG, 'psG') if hl == 0 else (psD, 'psD')
                    p.op('tensor', (lambda cc, hh: lambda e: e.transpose(
                        psH[:, 128 + 256 * hh:256 + 256 * hh],
                        Bb[hh][:, 3 + cc * 128:3 + cc * 128 + 128],
                        ident_bf[:, :]))(c, hl),
                        reads=[bkey, 'identb'], writes=['psH'])
                    p.op('vector', (lambda hh: lambda e: e.tensor_copy(
                        btm[hh][:, :], psH[:, 128 + 256 * hh:256 + 256 * hh]))(hl),
                        reads=['psH', f'btm{hl}'], writes=[f'btm{hl}'])
                    if full:
                        p.op('vector', (lambda cc, hh, hg, kp: lambda e: e.tensor_scalar(
                            out=dtx1[hh][:, :], in0=xctm[kp][:, hh * 64:(hh + 1) * 64],
                            scalar1=dtlm[:, cc * 96 + hg:cc * 96 + hg + 1],
                            scalar2=None, op0=OP.mult))(c, hl, h, cp),
                            reads=[f'xctm{cp}', f'dtlm_{c}', f'dtx1_{hl}'],
                            writes=[f'dtx1_{hl}'])
                    p.op('vector', (lambda cc, hh, hg, kp: lambda e: e.tensor_scalar(
                        out=dtx2[hh][:, :], in0=xctm[kp][:, hh * 64:(hh + 1) * 64],
                        scalar1=dtlm[:, cc * 96 + hg:cc * 96 + hg + 1],
                        scalar2=dtlm[:, cc * 96 + 64 + hg:cc * 96 + 64 + hg + 1],
                        op0=OP.mult, op1=OP.mult))(c, hl, h, cp),
                        reads=[f'xctm{cp}', f'dtlm_{c}', f'dtx2_{hl}'],
                        writes=[f'dtx2_{hl}'])
                    if full:
                        p.op('tensor', (lambda cc, hg, pg: lambda e: e.transpose(
                            pg[0:1, 0:128],
                            dtlm[:, cc * 96 + 32 + hg:cc * 96 + 32 + hg + 1],
                            ident_f[:, :]))(c, h, spg),
                            reads=[f'dtlm_{c}', 'identf'], writes=[spgk])
                        p.op('vector', (lambda hh, pg: lambda e: e.tensor_copy(
                            Lrow[hh][0:1, :], pg[0:1, 0:128]))(hl, spg),
                            reads=[spgk, f'Lrow{hl}'], writes=[f'Lrow{hl}'])
                        p.op('scalar', (lambda hh: lambda e: e.activation(
                            out=eLrow[hh][0:1, :], in_=Lrow[hh][0:1, :],
                            func=AF.Exp))(hl),
                            reads=[f'Lrow{hl}', f'eLrow{hl}'], writes=[f'eLrow{hl}'])
                        p.op('tensor', (lambda hh: lambda e: e.matmul(
                            psE[:, 0:128], ones_row_f[0:1, :], Lrow[hh][0:1, :],
                            start=True, stop=True))(hl),
                            reads=[f'Lrow{hl}', 'onesrf'], writes=['psE'])
                        p.op('tensor', (lambda hh: lambda e: e.matmul(
                            psF[:, 0:128], ones_row_f[0:1, :], eLrow[hh][0:1, :],
                            start=True, stop=True))(hl),
                            reads=[f'eLrow{hl}', 'onesrf'], writes=['psF'])
                        p.op('vector', (lambda hh: lambda e: e.tensor_copy(
                            eLtb[hh][:, :], psF[:, 0:128]))(hl),
                            reads=['psF', f'eLtb{hl}'], writes=[f'eLtb{hl}'])
                        p.op('tensor', (lambda cc, hh, pg: lambda e: e.matmul(
                            pg[:, 0:128], Bb[hh][:, 3 + cc * 128:3 + cc * 128 + 128],
                            Cb[hh][:, 3 + cc * 128:3 + cc * 128 + 128],
                            start=True, stop=True))(c, hl, spg),
                            reads=[bkey, ckey, f'Lrow{hl}'], writes=[spgk])
                        p.op('vector', (lambda cc, hg, hh: lambda e: e.scalar_tensor_tensor(
                            out=Dp[hh][:, :], in0=psE[:, 0:128],
                            scalar=dtlm[:, cc * 96 + 32 + hg:cc * 96 + 32 + hg + 1],
                            in1=maskT[:, :], op0=OP.subtract, op1=OP.add))(c, h, hl),
                            reads=['psE', f'dtlm_{c}', 'maskT', f'Dp{hl}'],
                            writes=[f'Dp{hl}'])
                        p.op('scalar', (lambda hh: lambda e: e.activation(
                            out=Mp[hh][:, :], in_=Dp[hh][:, :], func=AF.Exp))(hl),
                            reads=[f'Dp{hl}', f'Mp{hl}'], writes=[f'Mp{hl}'])
                        p.op('vector', (lambda hh, pg: lambda e: e.tensor_tensor(
                            out=Gt[hh][:, :], in0=pg[:, 0:128], in1=Mp[hh][:, :],
                            op=OP.mult))(hl, spg),
                            reads=[spgk, f'Mp{hl}', f'Gt{hl}'], writes=[f'Gt{hl}'])
                        p.op('vector', (lambda cc, hh: lambda e: e.tensor_tensor(
                            out=Cb[hh][:, 3 + cc * 128:3 + cc * 128 + 128],
                            in0=Cb[hh][:, 3 + cc * 128:3 + cc * 128 + 128],
                            in1=eLtb[hh][:, :], op=OP.mult))(c, hl),
                            reads=[ckey, f'eLtb{hl}'], writes=[ckey])
                        ypb, ypk = (psA, 'psA') if hl == 0 else (psC, 'psC')
                        p.op('tensor', (lambda hh, pb2_: lambda e: e.matmul(
                            pb2_[0:64, 0:128], dtx1[hh][:, :], Gt[hh][:, :],
                            start=True, stop=False))(hl, ypb),
                            reads=[f'dtx1_{hl}', f'Gt{hl}'], writes=[ypk])
                        p.op('tensor', (lambda cc, hh, pb2_, kp: lambda e: e.matmul(
                            pb2_[0:64, 0:128],
                            Hb2[kp][:, hh * 64:(hh + 1) * 64],
                            Cb[hh][:, 3 + cc * 128:3 + cc * 128 + 128],
                            start=False, stop=True))(c, hl, ypb, cp),
                            reads=[f'Hb{cp}', ckey], writes=[ypk])
                    p.op('tensor', (lambda hh: lambda e: e.matmul(
                        psB[:, hh * 64:(hh + 1) * 64], btm[hh][:, :],
                        dtx2[hh][:, :], start=True, stop=True))(hl),
                        reads=[f'btm{hl}', f'dtx2_{hl}'], writes=['psB'])
                p.op('vector', (lambda ii, hg0: lambda e: e.tensor_tensor(
                    out=H[:, ii * 128:(ii + 1) * 128].rearrange(
                        "p (a b) -> p a b", a=2),
                    in0=H[:, ii * 128:(ii + 1) * 128].rearrange(
                        "p (a b) -> p a b", a=2),
                    in1=alphab[:, hg0:hg0 + 2].unsqueeze(2).broadcast_to(
                        [128, 2, 64]),
                    op=OP.mult))(i, h0),
                    reads=[f'H_{i}', 'alphab'], writes=[f'H_{i}'])
                p.op('vector', (lambda ii: lambda e: e.tensor_tensor(
                    out=H[:, ii * 128:(ii + 1) * 128],
                    in0=H[:, ii * 128:(ii + 1) * 128],
                    in1=psB[:, 0:128], op=OP.add))(i),
                    reads=[f'H_{i}', 'psB'], writes=[f'H_{i}'])
                if full:
                    p.op('scalar', (lambda cc, ii: lambda e: e.activation(
                        out=Yt[ii][0:64, cc * 128:(cc + 1) * 128],
                        in_=psA[0:64, 0:128],
                        func=AF.Copy))(c, i), reads=['psA'], writes=[f'Yt{i}'])
                    p.op('scalar', (lambda cc, ii: lambda e: e.activation(
                        out=Yt[ii][64:128, cc * 128:(cc + 1) * 128],
                        in_=psC[0:64, 0:128],
                        func=AF.Copy))(c, i), reads=['psC'], writes=[f'Yt{i}'])

        def rms_gate_pair(i):
            sqscr = wpbq
            rmsr = dtsc2[0:2, 0:512]
            p.op('scalar', (lambda ii: lambda e: e.activation(
                out=sqscr[:, :], in_=Yt[ii][:, :], func=AF.Square))(i),
                reads=[f'Yt{i}', 'wpbq'], writes=['wpbq'])
            for j in range(MT):
                p.op('tensor', (lambda jj: lambda e: e.matmul(
                    psE[0:2, 0:512], blk2[:, :], sqscr[:, jj * 512:(jj + 1) * 512],
                    start=True, stop=True))(j),
                    reads=['wpbq', 'blk2'], writes=['psE'])
                p.op('vector', lambda e: e.tensor_copy(rmsr[:, :], psE[0:2, 0:512]),
                     reads=['psE', 'dtsc2'], writes=['dtsc2'])
                p.op('scalar', lambda e: e.activation(out=rmsr[:, :], in_=rmsr[:, :],
                     func=AF.Sqrt, scale=1.0 / 64, bias=eps6[0:2, 0:1]),
                     reads=['dtsc2', 'eps6'], writes=['dtsc2'])
                p.op('vector', lambda e: e.reciprocal(rmsr[:, :], rmsr[:, :]),
                     reads=['dtsc2'], writes=['dtsc2'])
                p.op('tensor', lambda e: e.matmul(
                    psE[:, 0:512], bcast2f[:, :], rmsr[:, :],
                    start=True, stop=True),
                    reads=['dtsc2', 'bc2'], writes=['psE'])
                p.op('vector', (lambda jj, ii: lambda e: e.tensor_tensor(
                    out=tg[:, :], in0=Yt[ii][:, jj * 512:(jj + 1) * 512],
                    in1=psE[:, 0:512], op=OP.mult))(j, i),
                    reads=[f'Yt{i}', 'psE'], writes=['tg'])
                p.op('vector', (lambda jj, ii: lambda e: e.tensor_tensor(
                    out=Yt[ii][:, jj * 512:(jj + 1) * 512], in0=tg[:, :],
                    in1=zsil[:, jj * 512:(jj + 1) * 512], op=OP.mult))(j, i),
                    reads=['tg', 'zsil'], writes=[f'Yt{i}'])

        if prefix_mode == 'flag':
            load_x(xp_d, 'xpd')
            dt_path()
            chunk_pre()
            p.op(g, lambda e: e.memset(xsb[:, 0:3], 0.0), writes=['xsb_h'])
            p.op(g, lambda e: e.memset(Bb[0][:, 0:3], 0.0), writes=['Bb0_h'])
            p.op(g, lambda e: e.memset(Bb[1][:, 0:3], 0.0), writes=['Bb1_h'])
            for i in range(NPAIR):
                proj_pair(i, [0, 2, 3], halo=False)
                conv_pair(i, [(xsb, 'xsb', cwxr, 'cwxr', i * 4),
                              (Bb[0], 'Bb0', cwbr, 'cwbr', (2 * i) * 4),
                              (Bb[1], 'Bb1', cwbr, 'cwbr', (2 * i + 1) * 4)])
                ssd_pair(i, full=False)
            for i in range(NPAIR):
                p.op('vector', (lambda ii: lambda e: e.tensor_scalar(
                    out=H[:, ii * 128:(ii + 1) * 128],
                    in0=H[:, ii * 128:(ii + 1) * 128],
                    scalar1=flagc[:, 0:1], scalar2=None, op0=OP.mult))(i),
                    reads=[f'H_{i}', 'flagc'], writes=[f'H_{i}'])

        load_x(xm_d, 'xmd')
        dt_path()
        chunk_pre()
        p.op(g, lambda e: e.memset(xsb[:, 0:3], 0.0),
             writes=['xsb_h'])
        p.op(g, lambda e: e.memset(Bb[0][:, 0:3], 0.0), writes=['Bb0_h'])
        p.op(g, lambda e: e.memset(Bb[1][:, 0:3], 0.0), writes=['Bb1_h'])
        p.op(g, lambda e: e.memset(Cb[0][:, 0:3], 0.0), writes=['Cb0_h'])
        p.op(g, lambda e: e.memset(Cb[1][:, 0:3], 0.0), writes=['Cb1_h'])
        if 'dt' in debug_outputs:
            p.dma('scalar', io['dbg_dtle'][:, :], dtLE[:, :],
                  reads=['dtLE_dt', 'dtLE_L', 'dtLE_ef'], writes=['dbgdt'])
        for i in range(NPAIR):
            proj_pair(i, [0, 1, 2, 3, 4, 5], halo=True)
            if i == 0 and 'proj0' in debug_outputs:
                allt = [f'_t{t}' for t in range(MT)] + ['_h']
                p.dma('scalar', io['dbg_pxsb'][:, :], xsb[:, :],
                      reads=['xsb' + t for t in allt], writes=['dbg5'])
                p.dma('scalar', io['dbg_pbb0'][:, :], Bb[0][:, :],
                      reads=['Bb0' + t for t in allt], writes=['dbg6'])
            conv_pair(i, [(xsb, 'xsb', cwxr, 'cwxr', i * 4),
                          (Bb[0], 'Bb0', cwbr, 'cwbr', (2 * i) * 4),
                          (Bb[1], 'Bb1', cwbr, 'cwbr', (2 * i + 1) * 4),
                          (Cb[0], 'Cb0', cwcr, 'cwcr', (2 * i) * 4),
                          (Cb[1], 'Cb1', cwcr, 'cwcr', (2 * i + 1) * 4)])
            if i == 0 and 'conv0' in debug_outputs:
                allt = [f'_t{t}' for t in range(MT)] + ['_h']
                p.dma('scalar', io['dbg_xsb'][:, :], xsb[:, :],
                      reads=['xsb' + t for t in allt], writes=['dbg1'])
                p.dma('scalar', io['dbg_bb0'][:, :], Bb[0][:, :],
                      reads=['Bb0' + t for t in allt], writes=['dbg2'])
                p.dma('scalar', io['dbg_cb0'][:, :], Cb[0][:, :],
                      reads=['Cb0' + t for t in allt], writes=['dbg3'])
                p.dma('scalar', io['dbg_zsil'][:, :], zsil[:, :],
                      reads=['zsil' + f'_t{t}' for t in range(MT)], writes=['dbg4'])
            ssd_pair(i, full=True)
            if i == 0 and 'yraw' in debug_outputs:
                p.dma('scalar', io['dbg_yraw'][:, :], Yt[0][:, :],
                      reads=['Yt0'], writes=['dbg7'])
            rms_gate_pair(i)
        if 'hend' in debug_outputs:
            p.dma('scalar', io['hend'][:, :], H[:, :],
                  reads=[f'H_{j}' for j in range(NPAIR)], writes=['hendout'])
        if 'ymix' in debug_outputs:
            for i in range(16):
                p.dma('scalar', io['ymix'][i * 128:(i + 1) * 128, :], Yt[i][:, :],
                      reads=[f'Yt{i}'], writes=['ymixout'])

        p.fence()
        m_stack.close()
        w_stack = ExitStack()
        wsb = mksb(w_stack)
        wq = [wsb("wq0", [128, 2048], BF16), wsb("wq1", [128, 2048], BF16)]
        xres = [wsb("xres0", [128, 512], F32), wsb("xres1", [128, 512], F32)]
        pre = [wsb(f"pre{i}", [128, 512], F32) for i in range(16)]
        x2sb = wsb("x2sb", [128, 512], BF16)
        sqbf = wsb("sqbf", [128, 512], BF16)
        prebf = wsb("prebf", [128, 512], BF16)
        rowm = wsb("rowm", [1, 512], F32)
        rowv = wsb("rowv", [1, 512], F32)
        rowr = wsb("rowr", [1, 512], F32)
        rowmr = wsb("rowmr", [1, 512], F32)

        def ln_apply(tc, pret, prek, psq, psqk, psm, psmk):
            for dm in range(16):
                p.op('scalar', (lambda d: lambda e: e.activation(
                    out=sqbf[:, :], in_=pret[d][:, :], func=AF.Square))(dm),
                    reads=[f'{prek}{dm}'], writes=['sqbf'])
                p.op('tensor', (lambda d: lambda e: e.matmul(
                    psq[0:1, 0:512], ones_col_b[:, 0:1], sqbf[:, :],
                    start=(d == 0), stop=(d == 15)))(dm),
                    reads=['sqbf', 'onescb'], writes=[psqk])
                p.op('scalar', (lambda d: lambda e: e.activation(
                    out=prebf[:, :], in_=pret[d][:, :], func=AF.Copy))(dm),
                    reads=[f'{prek}{dm}'], writes=['prebf'])
                p.op('tensor', (lambda d: lambda e: e.matmul(
                    psm[0:1, 0:512], ones_col_b[:, 0:1], prebf[:, :],
                    start=(d == 0), stop=(d == 15)))(dm),
                    reads=['prebf', 'onescb'], writes=[psmk])
            p.op('vector', lambda e: e.tensor_scalar(
                out=rowm[0:1, :], in0=psm[0:1, 0:512], scalar1=1.0 / D,
                scalar2=None, op0=OP.mult), reads=[psmk], writes=['rowm'])
            p.op('vector', lambda e: e.tensor_scalar(
                out=rowv[0:1, :], in0=psq[0:1, 0:512], scalar1=1.0 / D,
                scalar2=None, op0=OP.mult), reads=[psqk], writes=['rowv'])
            p.op('vector', lambda e: e.tensor_tensor(
                out=rowr[0:1, :], in0=rowm[0:1, :], in1=rowm[0:1, :], op=OP.mult),
                reads=['rowm'], writes=['rowr'])
            p.op('vector', lambda e: e.tensor_tensor(
                out=rowv[0:1, :], in0=rowv[0:1, :], in1=rowr[0:1, :],
                op=OP.subtract), reads=['rowv', 'rowr'], writes=['rowv'])
            p.op('scalar', lambda e: e.activation(out=rowv[0:1, :],
                 in_=rowv[0:1, :], func=AF.Sqrt, bias=eps5[0:1, 0:1]),
                 reads=['rowv', 'eps5'], writes=['rowv'])
            p.op('vector', lambda e: e.reciprocal(rowr[0:1, :], rowv[0:1, :]),
                 reads=['rowv'], writes=['rowr'])
            p.op('vector', lambda e: e.tensor_tensor(
                out=rowmr[0:1, :], in0=rowm[0:1, :], in1=rowr[0:1, :], op=OP.mult),
                reads=['rowm', 'rowr'], writes=['rowmr'])
            p.op('tensor', lambda e: e.matmul(psq[:, 0:512], ones_row_f[0:1, :],
                 rowr[0:1, :], start=True, stop=True),
                 reads=['rowr', 'onesrf'], writes=[psqk])
            p.op('tensor', lambda e: e.matmul(psm[:, 0:512], ones_row_f[0:1, :],
                 rowmr[0:1, :], start=True, stop=True),
                 reads=['rowmr', 'onesrf'], writes=[psmk])
            for dm in range(16):
                p.op('vector', (lambda d: lambda e: e.tensor_tensor(
                    out=pret[d][:, :], in0=pret[d][:, :], in1=psq[:, 0:512],
                    op=OP.mult))(dm),
                    reads=[f'{prek}{dm}', psqk], writes=[f'{prek}{dm}'])
                p.op('vector', (lambda d: lambda e: e.tensor_tensor(
                    out=x2sb[:, :], in0=pret[d][:, :],
                    in1=psm[:, 0:512], op=OP.subtract))(dm),
                    reads=[f'{prek}{dm}', psmk, 'x2sb'], writes=['x2sb'])
                p.dma('scalar', x2_d[dm, :, tc * 512:(tc + 1) * 512], x2sb[:, :],
                      reads=['x2sb'], writes=[f'x2d_{dm}'])

        for tc in range(MT):
            for dm in range(16):
                b = dm % 2
                p.dma('sync', wq[b][:, :], woq_d[dm, :, :],
                      reads=[f'woqd_{dm}', f'wq{b}'], writes=[f'wq{b}'])
                p.dma('scalar', xres[b][:, :],
                      io['xmain'][dm * 128:(dm + 1) * 128,
                                  3 + tc * 512:3 + tc * 512 + 512],
                      reads=[f'xres{b}'], writes=[f'xres{b}'])
                pb, pk = (psA, 'psA') if dm % 2 == 0 else (psB, 'psB')
                for c in range(16):
                    p.op('tensor', (lambda cc, bb, pbb, t: lambda e: e.matmul(
                        pbb[:, 0:512], wq[bb][:, cc * 128:(cc + 1) * 128],
                        Yt[cc][:, t * 512:(t + 1) * 512],
                        start=(cc == 0), stop=(cc == 15)))(c, b, pb, tc),
                        reads=[f'wq{b}', f'Yt{c}'], writes=[pk])
                p.op('vector', (lambda d, bb, pbb, pkk: lambda e: e.tensor_tensor(
                    out=pre[d][:, :], in0=pbb[:, 0:512], in1=xres[bb][:, :],
                    op=OP.add))(dm, b, pb, pk),
                    reads=[pk, f'xres{b}'], writes=[f'pre{dm}'])
            ln_apply(tc, pre, 'pre', psC, 'psC', psD, 'psD')
        w_stack.close()
        y_stack.close()
    else:
        x_stack = ExitStack()
        x2ldb = x_stack.enter_context(nc.sbuf_tensor("x2ldb", [128, M], F32))
        x2cst = x_stack.enter_context(nc.sbuf_tensor("x2cst", [128, M], BF16))
        for i in range(16):
            p.dma('sync', x2ldb[:, :], io['x2t'][i * 128:(i + 1) * 128, :],
                  reads=['x2ldb'], writes=['x2ldb'])
            p.op('vector', lambda e: e.tensor_copy(x2cst[:, :], x2ldb[:, :]),
                 reads=['x2ldb', 'x2cst'], writes=['x2cst'])
            p.dma('scalar', x2_d[i, :, :], x2cst[:, :],
                  reads=['x2cst'], writes=[f'x2d_{i}'])
        x_stack.close()
        m_stack.close()
        y_stack.close()

    if do_ffn:
        p.fence()
        f_stack = ExitStack()
        fsb = mksb(f_stack)
        x2bf = [fsb(f"x2bf{i}", [128, M], BF16) for i in range(16)]
        b1cs = fsb("b1cs", [128, 64], F32)
        b2cs = fsb("b2cs", [128, 16], F32)
        for i in range(16):
            p.dma('sync', x2bf[i][:, :], x2_d[i, :, :],
                  reads=[f'x2d_{i}'], writes=[f'x2bf{i}'])
        f1_stack = ExitStack()
        f1sb = mksb(f1_stack)
        w1b = [f1sb("w1b0", [128, 2048], F32), f1sb("w1b1", [128, 2048], F32)]
        w1mn = f1sb("w1mn", [128, 2048], BF16)
        w1q = f1sb("w1q", [128, 2048], BF16)
        hsb = [f1sb("hsb0", [128, 512], BF16), f1sb("hsb1", [128, 512], BF16)]
        p.dma('sync', b1cs[:, :], io['b1c'][:, :], writes=['b1cs'])
        p.dma('sync', b2cs[:, :], io['b2c'][:, :], writes=['b2cs'])
        for ft in range(64):
            b = ft % 2
            p.dma('sync', w1b[b][:, :], io['w1t'][ft, :, :],
                  reads=[f'w1b{b}'], writes=[f'w1b{b}'])
            p.op('vector', (lambda bb: lambda e: e.tensor_scalar(
                out=w1mn[:, :], in0=w1b[bb][:, :], scalar1=thr['n1'][:, 0:1],
                scalar2=None, op0=OP.is_le))(b),
                reads=[f'w1b{b}', 'thr_n1'], writes=['w1mn'])
            p.op('vector', (lambda bb: lambda e: e.scalar_tensor_tensor(
                out=w1q[:, :], in0=w1b[bb][:, :], scalar=thr['t1'][:, 0:1],
                in1=w1mn[:, :], op0=OP.is_ge, op1=OP.subtract))(b),
                reads=[f'w1b{b}', 'thr_t1', 'w1mn'], writes=['w1q'])
            for tc in range(MT):
                pb, pk = (psA, 'psA') if tc % 2 == 0 else (psB, 'psB')
                for c in range(NDT):
                    p.op('tensor', (lambda cc, t, pbb: lambda e: e.matmul(
                        pbb[:, 0:512], w1q[:, cc * 128:(cc + 1) * 128],
                        x2bf[cc][:, t * 512:(t + 1) * 512],
                        start=(cc == 0), stop=(cc == NDT - 1)))(c, tc, pb),
                        reads=['w1q', f'x2bf{c}'], writes=[pk])
                hb = tc % 2
                p.op('scalar', (lambda f_, pbb, bb: lambda e: e.activation(
                    out=hsb[bb][:, :], in_=pbb[:, 0:512], func=AF.Gelu,
                    bias=b1cs[:, f_:f_ + 1]))(ft, pb, hb),
                    reads=[pk, 'b1cs', f'hsb{hb}'], writes=[f'hsb{hb}'])
                p.dma('scalar', hscr_d[:, tc, ft, :], hsb[hb][:, :],
                      reads=[f'hsb{hb}'], writes=[f'hscr_{tc}_{ft}'])
        p.fence()
        f1_stack.close()
        f2_stack = ExitStack()
        f2sb = mksb(f2_stack)
        hslab = f2sb("hslab", [128, 64 * 512], BF16)
        w2qb = [f2sb("w2qb0", [128, 4096], BF16), f2sb("w2qb1", [128, 4096], BF16)]
        pre2 = [f2sb(f"pre2_{i}", [128, 512], F32) for i in range(16)]
        out_sb = f2sb("out_sb", [128, 512], F32)
        sq2 = f2sb("sq2", [128, 512], BF16)
        pb2 = f2sb("pb2", [128, 512], BF16)
        r2m = f2sb("r2m", [1, 512], F32)
        r2v = f2sb("r2v", [1, 512], F32)
        r2r = f2sb("r2r", [1, 512], F32)
        r2mr = f2sb("r2mr", [1, 512], F32)

        def ln2_apply(tc):
            for dm in range(16):
                p.op('scalar', (lambda d: lambda e: e.activation(
                    out=sq2[:, :], in_=pre2[d][:, :], func=AF.Square))(dm),
                    reads=[f'pre2_{dm}'], writes=['sq2'])
                p.op('tensor', (lambda d: lambda e: e.matmul(
                    psC[0:1, 0:512], ones_col_b[:, 0:1], sq2[:, :],
                    start=(d == 0), stop=(d == 15)))(dm),
                    reads=['sq2', 'onescb'], writes=['psC'])
                p.op('scalar', (lambda d: lambda e: e.activation(
                    out=pb2[:, :], in_=pre2[d][:, :], func=AF.Copy))(dm),
                    reads=[f'pre2_{dm}'], writes=['pb2'])
                p.op('tensor', (lambda d: lambda e: e.matmul(
                    psD[0:1, 0:512], ones_col_b[:, 0:1], pb2[:, :],
                    start=(d == 0), stop=(d == 15)))(dm),
                    reads=['pb2', 'onescb'], writes=['psD'])
            p.op('vector', lambda e: e.tensor_scalar(
                out=r2m[0:1, :], in0=psD[0:1, 0:512], scalar1=1.0 / D,
                scalar2=None, op0=OP.mult), reads=['psD'], writes=['r2m'])
            p.op('vector', lambda e: e.tensor_scalar(
                out=r2v[0:1, :], in0=psC[0:1, 0:512], scalar1=1.0 / D,
                scalar2=None, op0=OP.mult), reads=['psC'], writes=['r2v'])
            p.op('vector', lambda e: e.tensor_tensor(
                out=r2r[0:1, :], in0=r2m[0:1, :], in1=r2m[0:1, :], op=OP.mult),
                reads=['r2m'], writes=['r2r'])
            p.op('vector', lambda e: e.tensor_tensor(
                out=r2v[0:1, :], in0=r2v[0:1, :], in1=r2r[0:1, :],
                op=OP.subtract), reads=['r2v', 'r2r'], writes=['r2v'])
            p.op('scalar', lambda e: e.activation(out=r2v[0:1, :], in_=r2v[0:1, :],
                 func=AF.Sqrt, bias=eps5[0:1, 0:1]),
                 reads=['r2v', 'eps5'], writes=['r2v'])
            p.op('vector', lambda e: e.reciprocal(r2r[0:1, :], r2v[0:1, :]),
                 reads=['r2v'], writes=['r2r'])
            p.op('vector', lambda e: e.tensor_tensor(
                out=r2mr[0:1, :], in0=r2m[0:1, :], in1=r2r[0:1, :], op=OP.mult),
                reads=['r2m', 'r2r'], writes=['r2mr'])
            p.op('tensor', lambda e: e.matmul(psC[:, 0:512], ones_row_f[0:1, :],
                 r2r[0:1, :], start=True, stop=True),
                 reads=['r2r', 'onesrf'], writes=['psC'])
            p.op('tensor', lambda e: e.matmul(psD[:, 0:512], ones_row_f[0:1, :],
                 r2mr[0:1, :], start=True, stop=True),
                 reads=['r2mr', 'onesrf'], writes=['psD'])
            for dm in range(16):
                p.op('vector', (lambda d: lambda e: e.tensor_tensor(
                    out=pre2[d][:, :], in0=pre2[d][:, :], in1=psC[:, 0:512],
                    op=OP.mult))(dm),
                    reads=[f'pre2_{dm}', 'psC'], writes=[f'pre2_{dm}'])
                p.op('vector', (lambda d: lambda e: e.tensor_tensor(
                    out=out_sb[:, :], in0=pre2[d][:, :], in1=psD[:, 0:512],
                    op=OP.subtract))(dm),
                    reads=[f'pre2_{dm}', 'psD', 'out_sb'], writes=['out_sb'])
                p.dma('scalar', io['out_t'][dm * 128:(dm + 1) * 128,
                      tc * 512:(tc + 1) * 512], out_sb[:, :],
                      reads=['out_sb'], writes=[f'outd_{dm}_{tc}'])

        for tc in range(MT):
            p.dma('sync', hslab[:, :],
                  hscr_d[:, tc, :, :].rearrange("p a b -> p (a b)"),
                  reads=[f'hscr_{tc}_{ft}' for ft in range(64)] + ['hslab'],
                  writes=['hslab'])
            for dm in range(16):
                pb, pk = (psA, 'psA') if dm % 2 == 0 else (psB, 'psB')
                for half in range(2):
                    hb = half
                    p.dma('sync', w2qb[hb][:, :],
                          w2q_d[dm, :, half * 4096:(half + 1) * 4096],
                          reads=[f'w2qd_{dm}', f'w2qb{hb}'], writes=[f'w2qb{hb}'])
                    for c in range(32):
                        cg = half * 32 + c
                        p.op('tensor', (lambda cc, cg_, hh, pbb: lambda e: e.matmul(
                            pbb[:, 0:512], w2qb[hh][:, cc * 128:(cc + 1) * 128],
                            hslab[:, cg_ * 512:(cg_ + 1) * 512],
                            start=(cg_ == 0), stop=(cg_ == 63)))(c, cg, hb, pb),
                            reads=[f'w2qb{hb}', 'hslab'], writes=[pk])
                p.op('vector', (lambda d, t, pbb, pkk: lambda e:
                     e.scalar_tensor_tensor(
                         out=pre2[d][:, :], in0=pbb[:, 0:512],
                         scalar=b2cs[:, d:d + 1],
                         in1=x2bf[d][:, t * 512:(t + 1) * 512].bitcast(F32)
                         if False else x2bf[d][:, t * 512:(t + 1) * 512],
                         op0=OP.add, op1=OP.add))(dm, tc, pb, pk),
                    reads=[pk, 'b2cs', f'x2bf{dm}'], writes=[f'pre2_{dm}'])
            ln2_apply(tc)
        f2_stack.close()
        f_stack.close()
    else:
        o_stack = ExitStack()
        ocst = o_stack.enter_context(nc.sbuf_tensor("ocst", [128, M], BF16))
        ocast = o_stack.enter_context(nc.sbuf_tensor("ocast", [128, M], F32))
        for i in range(16):
            p.dma('sync', ocst[:, :], x2_d[i, :, :],
                  reads=[f'x2d_{i}', 'ocst'], writes=['ocst'])
            p.op('vector', lambda e: e.tensor_copy(ocast[:, :], ocst[:, :]),
                 reads=['ocst', 'ocast'], writes=['ocast'])
            p.dma('scalar', io['out_t'][i * 128:(i + 1) * 128, :], ocast[:, :],
                  reads=['ocast'], writes=['outd'])
        o_stack.close()

    p.finish()
    p.emit()
    g_stack.close()
    return nc


def layout_weights(ins):
    Win = np.ascontiguousarray(np.asarray(ins['L0_attn_Win']), dtype=np.float32)
    Wout = np.ascontiguousarray(np.asarray(ins['L0_attn_Wout']), dtype=np.float32)
    w1 = np.ascontiguousarray(np.asarray(ins['L0_ffn_w1']), dtype=np.float32)
    w2 = np.ascontiguousarray(np.asarray(ins['L0_ffn_w2']), dtype=np.float32)
    Win_T = np.ascontiguousarray(Win.T)
    HPo = NH * P
    winp = np.empty((NPAIR, 6, 128, NDT * 128), np.float32)
    for i in range(NPAIR):
        h0, h1 = 2 * i, 2 * i + 1
        cols = [
            np.r_[h0 * P + np.arange(P), h1 * P + np.arange(P)],
            np.r_[HPo + h0 * P + np.arange(P), HPo + h1 * P + np.arange(P)],
            2 * HPo + h0 * NS + np.arange(NS),
            2 * HPo + h1 * NS + np.arange(NS),
            2 * HPo + NH * NS + h0 * NS + np.arange(NS),
            2 * HPo + NH * NS + h1 * NS + np.arange(NS),
        ]
        for t, cl in enumerate(cols):
            blk = Win_T[:, cl]
            winp[i, t] = blk.reshape(NDT, 128, 128).transpose(1, 0, 2).reshape(
                128, NDT * 128)
    wdt = np.ascontiguousarray(
        Win_T[:, 2 * HPo + 2 * NH * NS:].reshape(NDT, 128, 32)
        .transpose(1, 0, 2).reshape(128, NDT * 32))
    woutt = np.ascontiguousarray(
        Wout.reshape(16, 128, 16, 128).transpose(0, 3, 2, 1).reshape(16, 128, 2048))
    w1t = np.ascontiguousarray(
        w1.reshape(64, 128, 16, 128).transpose(0, 3, 2, 1).reshape(64, 128,
                                                                   NDT * 128))
    w2t = np.ascontiguousarray(
        w2.reshape(16, 128, 64, 128).transpose(0, 3, 2, 1).reshape(16, 128,
                                                                   64 * 128))
    cwx = np.ascontiguousarray(
        np.asarray(ins['L0_attn_conv_w_x'], dtype=np.float32)
        .reshape(NPAIR, 128, KC).transpose(1, 0, 2).reshape(128, NPAIR * KC))
    cwb = np.ascontiguousarray(
        np.asarray(ins['L0_attn_conv_w_b'], dtype=np.float32)
        .transpose(1, 0, 2).reshape(128, NH * KC))
    cwc = np.ascontiguousarray(
        np.asarray(ins['L0_attn_conv_w_c'], dtype=np.float32)
        .transpose(1, 0, 2).reshape(128, NH * KC))
    b1c = np.ascontiguousarray(
        np.asarray(ins['L0_ffn_b1'], dtype=np.float32).reshape(64, 128).T)
    b2c = np.ascontiguousarray(
        np.asarray(ins['L0_ffn_b2'], dtype=np.float32).reshape(16, 128).T)
    alog = np.zeros((128, 1), np.float32)
    alog[0:32, 0] = np.asarray(ins['L0_attn_A_log'], dtype=np.float32)
    dtb = np.zeros((128, 1), np.float32)
    dtb[0:32, 0] = np.asarray(ins['L0_attn_dt_bias'], dtype=np.float32)
    return dict(winp=winp, wdt=wdt, woutt=woutt, w1t=w1t, w2t=w2t,
                cwx=cwx, cwb=cwb, cwc=cwc, b1c=b1c, b2c=b2c,
                alog=alog, dtb=dtb)


def layout_x_slice(x, b, t0, M):
    out = np.zeros((D, M + 3), np.float32)
    lo = max(0, t0 - 3)
    npad = lo - (t0 - 3)
    seg = np.ascontiguousarray(np.asarray(x[b, lo:t0 + M, :], dtype=np.float32).T)
    out[:, npad:] = seg
    return out


def kernel(**inputs):
    M = 2048
    w = layout_weights(inputs)
    x = np.asarray(inputs['L0_attn_x'])
    nc = build_nc(M=M, prefix_mode='flag')
    in_maps = []
    for core in range(NCORES):
        b, r = core // 2, core % 2
        im = dict(w)
        im['xmain'] = layout_x_slice(x, b, r * M, M)
        im['xpref'] = (layout_x_slice(x, b, 0, M) if r == 1
                       else np.zeros((D, M + 3), np.float32))
        fl = np.zeros((128, 1), np.float32)
        fl[:, 0] = float(r)
        im['flag'] = fl
        in_maps.append(im)
    res = run_bass_kernel_spmd(nc, in_maps, core_ids=list(range(NCORES)))
    out = np.empty((4, 4096, D), np.float32)
    for core in range(NCORES):
        b, r = core // 2, core % 2
        out[b, r * M:(r + 1) * M, :] = res.results[core]['out_t'].T
    return out


# version 11 (final)
# speedup vs baseline: 1.3323x; 1.3323x over previous
import numpy as np
from contextlib import ExitStack
import concourse.bass as bass
import concourse.mybir as mybir
from concourse.bass_utils import run_bass_kernel_spmd
from concourse.alu_op_type import AluOpType as OP

F32 = mybir.dt.float32
BF16 = mybir.dt.bfloat16
AF = mybir.ActivationFunctionType

D = 2048
NH = 32
P = 64
NS = 128
KC = 4
DFF = 8192
NCORES = 8
Q = 128
NPAIR = 16
NDT = 16

ENGINES = ("tensor", "vector", "scalar", "gpsimd", "sync")


class Prog:

    def __init__(self, nc, strict_dma=True):
        self.nc = nc
        self.strict_dma = strict_dma
        self.thunks = []
        self.ops = []
        self.ctick = {e: 0 for e in ENGINES}
        self.dtick = {e: 0 for e in ENGINES}
        self.last_w = {}
        self.reads = {}
        self.fence_dep = None
        self.fence_applied = set()

    def _need(self, cur_eng, cur_is_dma, dep):
        kind, eng, tick = dep
        if (eng == cur_eng and kind == 'c' and not cur_is_dma
                and eng == 'tensor'):
            return None
        if kind == 'd' and self.strict_dma:
            return ('d', eng, self.dtick[eng])
        return dep

    def op(self, engine, fn, reads=(), writes=(), is_dma=False):
        deps = set()
        if self.fence_dep is not None and engine not in self.fence_applied:
            deps.add(self.fence_dep)
            self.fence_applied.add(engine)
        for r in reads:
            d = self.last_w.get(r)
            if d is not None:
                d2 = self._need(engine, is_dma, d)
                if d2:
                    deps.add(d2)
        for w in writes:
            d = self.last_w.get(w)
            if d is not None:
                d2 = self._need(engine, is_dma, d)
                if d2:
                    deps.add(d2)
            for kd, tick in self.reads.get(w, {}).items():
                d2 = self._need(engine, is_dma, (kd[0], kd[1], tick))
                if d2:
                    deps.add(d2)
        waits = {}
        for kind, eng, tick in deps:
            kk = (kind, eng)
            waits[kk] = max(waits.get(kk, 0), tick)
        if is_dma:
            self.dtick[engine] += 1
            me = ('d', engine, self.dtick[engine])
        else:
            self.ctick[engine] += 1
            me = ('c', engine, self.ctick[engine])
        self.ops.append((engine, fn, is_dma, waits))
        for w in writes:
            self.last_w[w] = me
            self.reads[w] = {}
        for r in reads:
            dd = self.reads.setdefault(r, {})
            kk = (me[0], me[1])
            dd[kk] = max(dd.get(kk, 0), me[2])
        return me

    def dma(self, queue, out, in_, reads=(), writes=(), **kw):
        return self.op(queue, lambda e: e.dma_start(out=out, in_=in_, **kw),
                       reads=reads, writes=writes, is_dma=True)

    def defer(self, fn):
        self.thunks.append(fn)

    def pump(self, n=1):
        while n > 0 and self.thunks:
            self.thunks.pop(0)()
            n -= 1

    def drain(self):
        while self.thunks:
            self.thunks.pop(0)()

    def fence(self):
        waits = {}
        for e in ENGINES:
            if self.ctick[e] > 0:
                waits[('c', e)] = self.ctick[e]
            if self.dtick[e] > 0:
                waits[('d', e)] = self.dtick[e]
        self.ctick['gpsimd'] += 1
        me = ('c', 'gpsimd', self.ctick['gpsimd'])
        self.ops.append(('gpsimd', lambda e: e.nop(), False, waits))
        self.fence_dep = me
        self.fence_applied = {'gpsimd'}

    def finish(self):
        waits = {}
        for e in ENGINES:
            if self.ctick[e] > 0:
                waits[('c', e)] = self.ctick[e]
            if self.dtick[e] > 0:
                waits[('d', e)] = self.dtick[e]
        self.ops.append(('gpsimd', lambda e: e.nop(), False, waits))

    def emit(self):
        nc = self.nc
        with ExitStack() as stack:
            sems = {}
            for e in ENGINES:
                sems[('c', e)] = stack.enter_context(nc.semaphore(f"p_{e}"))
                sems[('d', e)] = stack.enter_context(nc.semaphore(f"pd_{e}"))
            block = stack.enter_context(nc.Block())
            plans = {e: [] for e in ENGINES}
            for engine, fn, is_dma, waits in self.ops:
                plans[engine].append((fn, is_dma, waits))

            def mk(engname):
                def body(eng):
                    for fn, is_dma, waits in plans[engname]:
                        for (kind, weng), tick in sorted(waits.items()):
                            eng.wait_ge(sems[(kind, weng)],
                                        tick * (16 if kind == 'd' else 1))
                        fn(eng).then_inc(
                            sems[('d' if is_dma else 'c', engname)],
                            16 if is_dma else 1)
                body.__name__ = f"body_{engname}"
                return body

            block.sync(mk('sync'))
            block.scalar(mk('scalar'))
            block.vector(mk('vector'))
            block.tensor(mk('tensor'))
            block.gpsimd(mk('gpsimd'))


def build_nc(M=2048, prefix_mode='flag', strict_dma=True, x2_input=False,
             do_ffn=True, debug_outputs=()):
    nc = bass.Bass("TRN2", target_bir_lowering=False, debug=False)
    MT = M // 512
    NCH = M // Q
    FW = M + 8

    def din(name, shape, dtype=F32):
        return nc.dram_tensor(name, shape, dtype, kind="ExternalInput").ap()

    def dout(name, shape, dtype=F32):
        return nc.dram_tensor(name, shape, dtype, kind="ExternalOutput").ap()

    io = {}
    if not x2_input:
        io['xmain'] = din("xmain", [D, M + 3])
        io['winp'] = din("winp", [NPAIR, 6, 128, NDT * 128])
        io['wdt'] = din("wdt", [128, NDT * 32])
        io['cwx'] = din("cwx", [128, NPAIR * KC])
        io['cwb'] = din("cwb", [128, NH * KC])
        io['cwc'] = din("cwc", [128, NH * KC])
        io['alog'] = din("alog", [128, 1])
        io['dtb'] = din("dtb", [128, 1])
        io['woutt'] = din("woutt", [16, 128, 2048])
        if prefix_mode == 'flag':
            io['xpref'] = din("xpref", [D, M + 3])
            io['flag'] = din("flag", [128, 1])
        elif prefix_mode == 'input':
            io['hinit'] = din("hinit", [128, NH * P])
    else:
        io['x2t'] = din("x2t", [D, M])
    if do_ffn:
        io['w1t'] = din("w1t", [64, 128, NDT * 128])
        io['b1c'] = din("b1c", [128, 64])
        io['w2t'] = din("w2t", [16, 128, 64 * 128])
        io['b2c'] = din("b2c", [128, 16])
    io['out_t'] = dout("out_t", [D, M])
    if 'hend' in debug_outputs:
        io['hend'] = dout("hend", [128, NH * P])
    if 'ymix' in debug_outputs:
        io['ymix'] = dout("ymix", [D, M], BF16)
    if 'dt' in debug_outputs:
        io['dbg_dtle'] = dout("dbg_dtle", [128, M])
    if 'yraw' in debug_outputs:
        io['dbg_yraw'] = dout("dbg_yraw", [128, M], BF16)
    if 'proj0' in debug_outputs:
        io['dbg_pxsb'] = dout("dbg_pxsb", [128, M + 8], BF16)
        io['dbg_pbb0'] = dout("dbg_pbb0", [128, M + 8], BF16)
    if 'conv0' in debug_outputs:
        io['dbg_xsb'] = dout("dbg_xsb", [128, M + 8], BF16)
        io['dbg_bb0'] = dout("dbg_bb0", [128, M + 8], BF16)
        io['dbg_cb0'] = dout("dbg_cb0", [128, M + 8], BF16)
        io['dbg_zsil'] = dout("dbg_zsil", [128, M], BF16)

    if do_ffn:
        w2q_d = nc.dram_tensor("w2q_d", [16, 128, 64 * 128], BF16).ap()
        hscr_d = nc.dram_tensor("hscr_d", [128, MT, 64, 512], BF16).ap()
    if not x2_input:
        woq_d = nc.dram_tensor("woq_d", [16, 128, 2048], BF16).ap()
        xm_d = nc.dram_tensor("xm_d", [D, M + 3], BF16).ap()
        if prefix_mode == 'flag':
            xp_d = nc.dram_tensor("xp_d", [D, M + 3], BF16).ap()
    x2_d = nc.dram_tensor("x2_d", [16, 128, M], BF16).ap()

    p = Prog(nc, strict_dma=strict_dma)
    g_stack = ExitStack()

    def mksb(stk):
        def sb(name, shape, dtype):
            return stk.enter_context(nc.sbuf_tensor(name, shape, dtype))
        return sb

    def mkps(stk):
        def psf(name, shape, dtype=F32):
            return stk.enter_context(nc.psum_tensor(name, shape, dtype))
        return psf

    sb = mksb(g_stack)
    psf = mkps(g_stack)

    ident_bf = sb("ident_bf", [128, 128], BF16)
    ident_f = sb("ident_f", [128, 128], F32)
    maskT = sb("maskT", [128, 128], F32)
    ones_row_f = sb("ones_row_f", [1, 128], F32)
    ones_col_b = sb("ones_col_b", [128, 1], BF16)
    ones_col_f = sb("ones_col_f", [128, 1], F32)
    blk2 = sb("blk2", [128, 2], BF16)
    blk2f = sb("blk2f", [128, 2], F32)
    bcast2f = sb("bcast2f", [2, 128], F32)
    eps5 = sb("eps5", [128, 1], F32)
    eps6 = sb("eps6", [128, 1], F32)
    psA = psf("psA", [128, 512])
    psB = psf("psB", [128, 512])
    psC = psf("psC", [128, 512])
    psD = psf("psD", [128, 512])

    g = 'gpsimd'
    p.op(g, lambda e: e.memset(ident_bf[:], 0.0), writes=['identb'])
    p.op(g, lambda e: e.affine_select(out=ident_bf[:], in_=ident_bf[:],
         compare_op=OP.not_equal, fill=1.0, base=0, pattern=[[-1, 128]],
         channel_multiplier=1), reads=['identb'], writes=['identb'])
    p.op(g, lambda e: e.memset(ident_f[:], 0.0), writes=['identf'])
    p.op(g, lambda e: e.affine_select(out=ident_f[:], in_=ident_f[:],
         compare_op=OP.not_equal, fill=1.0, base=0, pattern=[[-1, 128]],
         channel_multiplier=1), reads=['identf'], writes=['identf'])
    p.op(g, lambda e: e.memset(maskT[:], 0.0), writes=['maskT'])
    p.op(g, lambda e: e.affine_select(out=maskT[:], in_=maskT[:],
         compare_op=OP.is_ge, fill=-1e30, base=0, pattern=[[1, 128]],
         channel_multiplier=-1), reads=['maskT'], writes=['maskT'])
    p.op(g, lambda e: e.memset(ones_row_f[:], 1.0), writes=['onesrf'])
    p.op(g, lambda e: e.memset(eps5[:], 1e-5), writes=['eps5'])
    p.op(g, lambda e: e.memset(eps6[:], 1e-6), writes=['eps6'])
    p.op(g, lambda e: e.memset(ones_col_b[:], 1.0), writes=['onescb'])
    p.op(g, lambda e: e.memset(ones_col_f[:], 1.0), writes=['onescf'])
    p.op(g, lambda e: e.memset(blk2[:], 0.0), writes=['blk2'])
    p.op(g, lambda e: e.affine_select(out=blk2[:, 0:1], in_=blk2[:, 0:1],
         compare_op=OP.is_ge, fill=1.0, base=-64, pattern=[[0, 1]],
         channel_multiplier=1), reads=['blk2'], writes=['blk2'])
    p.op(g, lambda e: e.affine_select(out=blk2[:, 1:2], in_=blk2[:, 1:2],
         compare_op=OP.is_ge, fill=1.0, base=63, pattern=[[0, 1]],
         channel_multiplier=-1), reads=['blk2'], writes=['blk2'])
    p.op(g, lambda e: e.memset(blk2f[:], 0.0), writes=['blk2f'])
    p.op(g, lambda e: e.affine_select(out=blk2f[:, 0:1], in_=blk2f[:, 0:1],
         compare_op=OP.is_ge, fill=1.0, base=-64, pattern=[[0, 1]],
         channel_multiplier=1), reads=['blk2f'], writes=['blk2f'])
    p.op(g, lambda e: e.affine_select(out=blk2f[:, 1:2], in_=blk2f[:, 1:2],
         compare_op=OP.is_ge, fill=1.0, base=63, pattern=[[0, 1]],
         channel_multiplier=-1), reads=['blk2f'], writes=['blk2f'])
    p.op('tensor', lambda e: e.transpose(psA[0:2, 0:128], blk2f[:, 0:2],
         ident_f[:, :]), reads=['blk2f', 'identf'], writes=['psA'])
    p.op('vector', lambda e: e.tensor_copy(bcast2f[:, :], psA[0:2, 0:128]),
         reads=['psA'], writes=['bc2'])

    thr = {}
    if do_ffn:
        thr['t1'] = sb("thr1", [128, 1], F32)
        thr['n1'] = sb("nthr1", [128, 1], F32)
        thr['t2'] = sb("thr2", [128, 1], F32)
        thr['n2'] = sb("nthr2", [128, 1], F32)
    y_stack = ExitStack()
    if not x2_input:
        Yt = [y_stack.enter_context(nc.sbuf_tensor(f"Yt{i}", [128, M], BF16))
              for i in range(16)]
    m_stack = ExitStack()
    msb = mksb(m_stack)
    mps = mkps(m_stack)
    if do_ffn:
        sclbuf = msb("w2b", [128, 512], F32)
        sclbuf2 = msb("w2b2", [128, 512], F32)
        sclpart = msb("sclpart", [128, 2], F32)
        sclrow = msb("sclrow", [1, 2], F32)
        w2mn = msb("w2mn", [128, 512], BF16)
        w2qs = msb("w2qs", [128, 512], BF16)
        w2qs2 = msb("w2qs2", [128, 512], BF16)
        sbufs = [(sclbuf, 'sclb', w2qs, 'w2qs'), (sclbuf2, 'sclb2', w2qs2, 'w2qs2')]

        scnt = [0]

        def scale_pass(wt3d, nblk, blkw, tag, tkey, nkey, nelem):
            p.defer(lambda: p.op('vector', lambda e: e.memset(sclpart[:, 0:1], 0.0),
                    writes=[f'sp_{tag}']))
            nsub = blkw // 512

            def one(i, j):
                fb, fk, qb, qk = sbufs[scnt[0] % 2]
                scnt[0] += 1
                p.dma('gpsimd', fb[:, :], wt3d[i, :, j * 512:(j + 1) * 512],
                      reads=[fk], writes=[fk])
                p.op('vector', lambda e: e.tensor_reduce(
                    out=sclpart[:, 1:2], in_=fb[:, :],
                    axis=mybir.AxisListType.X, op=OP.add,
                    apply_absolute_value=True),
                    reads=[fk], writes=[f'sq_{tag}'])
                p.op('vector', lambda e: e.tensor_tensor(
                    out=sclpart[:, 0:1], in0=sclpart[:, 0:1],
                    in1=sclpart[:, 1:2], op=OP.add),
                    reads=[f'sq_{tag}', f'sp_{tag}'], writes=[f'sp_{tag}'])
            for i in range(nblk):
                for j in range(nsub):
                    p.defer((lambda a, b: lambda: one(a, b))(i, j))
            p.defer(lambda: scale_fin(tag, tkey, nkey, nelem))

        def scale_fin(tag, tkey, nkey, nelem):
            p.op('tensor', lambda e: e.matmul(psA[0:1, 0:1], ones_col_f[:, 0:1],
                 sclpart[:, 0:1], start=True, stop=True),
                 reads=[f'sp_{tag}', 'onescf'], writes=['psA'])
            p.op('vector', lambda e: e.tensor_copy(sclrow[0:1, 0:1], psA[0:1, 0:1]),
                 reads=['psA'], writes=[f'srow_{tag}'])
            p.op('tensor', lambda e: e.matmul(psA[:, 1:2], ones_row_f[0:1, :],
                 sclrow[0:1, 0:1], start=True, stop=True),
                 reads=[f'srow_{tag}', 'onesrf'], writes=['psA'])
            p.op('vector', lambda e: e.tensor_scalar(
                out=thr[tkey][:, 0:1], in0=psA[:, 1:2], scalar1=0.7 / nelem,
                scalar2=None, op0=OP.mult),
                reads=['psA'], writes=[f'thr_{tkey}'])
            p.op('vector', lambda e: e.tensor_scalar(
                out=thr[nkey][:, 0:1], in0=psA[:, 1:2], scalar1=-0.7 / nelem,
                scalar2=None, op0=OP.mult),
                reads=['psA'], writes=[f'thr_{nkey}'])

    if not x2_input:
        if not do_ffn:
            sclbuf = msb("w2b", [128, 512], F32)
            sclbuf2 = msb("w2b2", [128, 512], F32)
            w2qs = msb("w2qs", [128, 512], BF16)
            w2qs2 = msb("w2qs2", [128, 512], BF16)
            sbufs = [(sclbuf, 'sclb', w2qs, 'w2qs'), (sclbuf2, 'sclb2', w2qs2, 'w2qs2')]

        def x_prepass(src, dst, tag, thunked):
            cnt = [0]

            def one(ib, q4):
                fb, fk, qb, qk = sbufs[cnt[0] % 2]
                cnt[0] += 1
                lo = q4 * 512
                w_ = min(512, M + 3 - lo)
                p.dma('gpsimd', fb[:, 0:w_],
                      src[ib * 128:(ib + 1) * 128, lo:lo + w_],
                      reads=[fk], writes=[fk])
                p.op('vector', (lambda: lambda e: e.tensor_copy(
                    qb[:, 0:w_], fb[:, 0:w_]))(),
                    reads=[fk, qk], writes=[qk])
                p.dma('gpsimd', dst[ib * 128:(ib + 1) * 128, lo:lo + w_],
                      qb[:, 0:w_],
                      reads=[qk], writes=[f'{tag}_{ib}'])
            for ib in range(16):
                for q4 in range(5):
                    if q4 * 512 >= M + 3:
                        break
                    if thunked:
                        p.defer((lambda a, b: lambda: one(a, b))(ib, q4))
                    else:
                        one(ib, q4)

        if prefix_mode == 'flag':
            x_prepass(io['xpref'], xp_d, 'xpd', thunked=False)
        x_prepass(io['xmain'], xm_d, 'xmd', thunked=True)
        wcnt = [0]

        def woq_one(dm, half):
            fb, fk, qb, qk = sbufs[wcnt[0] % 2]
            wcnt[0] += 1
            sl = slice(half * 512, (half + 1) * 512)
            p.dma('gpsimd', fb[:, :], io['woutt'][dm, :, sl],
                  reads=[fk], writes=[fk])
            p.op('vector', lambda e: e.tensor_copy(qb[:, :], fb[:, :]),
                 reads=[fk, qk], writes=[qk])
            p.dma('gpsimd', woq_d[dm, :, sl], qb[:, :],
                  reads=[qk], writes=[f'woqd_{dm}'])
        for dm in range(16):
            for half in range(4):
                p.defer((lambda a, b: lambda: woq_one(a, b))(dm, half))

    if do_ffn:
        scale_pass(io['w1t'], 64, 2048, 'w1', 't1', 'n1', float(DFF * D))
        scale_pass(io['w2t'], 16, 8192, 'w2', 't2', 'n2', float(DFF * D))

        qcnt = [0]

        def w2q_one(dm, qr):
            fb, fk, qb, qk = sbufs[qcnt[0] % 2]
            qcnt[0] += 1
            sl = slice(qr * 512, (qr + 1) * 512)
            p.dma('gpsimd', fb[:, :], io['w2t'][dm, :, sl],
                  reads=[fk], writes=[fk])
            p.op('vector', lambda e: e.tensor_scalar(
                out=w2mn[:, :], in0=fb[:, :], scalar1=thr['n2'][:, 0:1],
                scalar2=None, op0=OP.is_le),
                reads=[fk, 'thr_n2', 'w2mn'], writes=['w2mn'])
            p.op('vector', lambda e: e.scalar_tensor_tensor(
                out=qb[:, :], in0=fb[:, :], scalar=thr['t2'][:, 0:1],
                in1=w2mn[:, :], op0=OP.is_ge, op1=OP.subtract),
                reads=[fk, 'thr_t2', 'w2mn', qk], writes=[qk])
            p.dma('gpsimd', w2q_d[dm, :, sl], qb[:, :],
                  reads=[qk], writes=[f'w2qd_{dm}'])
        for dm in range(16):
            for qr in range(16):
                p.defer((lambda a, b: lambda: w2q_one(a, b))(dm, qr))

    if not x2_input:
        psE = mps("psE", [128, 512])
        psF = mps("psF", [128, 512])
        psG = mps("psG", [128, 512])
        psH = mps("psH", [128, 1024], BF16)

        XT = [msb(f"XT{i}", [128, FW], BF16) for i in range(16)]
        wpb = msb("wpb", [128, 2048], F32)
        wpbq = msb("wpbq", [128, 2048], BF16)
        Acol = msb("Acol", [128, 1], F32)
        dtbc = msb("dtbc", [128, 1], F32)
        flagc = msb("flagc", [128, 1], F32)
        cwxr = msb("cwxr", [128, NPAIR * KC], F32)
        cwbr = msb("cwbr", [128, NH * KC], F32)
        cwcr = msb("cwcr", [128, NH * KC], F32)
        diag = [msb(f"diag{k}", [128, 128], BF16) for k in range(KC)]
        dtLE = msb("dtLE", [128, M], F32)
        dtsc2s = msb("dtsc2", [32, 512], F32)
        zeros32 = msb("zeros32", [32, 128], F32)
        dtlm = msb("dtlm", [128, NCH * 96], F32)
        H = msb("H", [128, NH * P], F32)
        alphab = msb("alphab", [128, 32], F32)
        xsb = msb("xsb", [128, FW], BF16)
        Bb = [msb("Bb0", [128, FW], BF16), msb("Bb1", [128, FW], BF16)]
        Cb = [msb("Cb0", [128, FW], BF16), msb("Cb1", [128, FW], BF16)]
        zsil = msb("zsil", [128, M], BF16)
        Lrow_ = msb("Lrow", [1, 128], F32)
        Lrow = [Lrow_, Lrow_]
        eLrow_ = msb("eLrow", [1, 128], F32)
        eLrow = [eLrow_, eLrow_]
        Dp_ = msb("Dp", [128, 128], F32)
        Dp = [Dp_, Dp_]
        Mp_ = msb("Mp", [128, 128], F32)
        Mp = [Mp_, Mp_]
        Gt = [msb("Gt0", [128, 128], BF16), msb("Gt1", [128, 128], BF16)]
        xctm = [msb("xctm0", [128, 128], BF16), msb("xctm1", [128, 128], BF16)]
        btm = [msb("btm0", [128, 128], BF16), msb("btm1", [128, 128], BF16)]
        dtx1 = [msb("dtx1_0", [128, 64], BF16), msb("dtx1_1", [128, 64], BF16)]
        dtx2 = [msb("dtx2_0", [128, 64], BF16), msb("dtx2_1", [128, 64], BF16)]
        eLtb_ = msb("eLtb", [128, 128], F32)
        eLtb = [eLtb_, eLtb_]
        expLs = msb("expLs", [128, 32], F32)
        Hb2 = [msb("Hb2_0", [128, 128], BF16), msb("Hb2_1", [128, 128], BF16)]
        tg = msb("tg", [128, 512], BF16)

        p.op(g, lambda e: e.memset(zeros32[:], 0.0), writes=['z32'])
        p.op(g, lambda e: e.memset(H[:], 0.0),
             writes=[f'H_{i}' for i in range(NPAIR)])

        p.dma('sync', cwxr[:, :], io['cwx'][:, :], writes=['cwxr'])
        p.dma('sync', cwbr[:, :], io['cwb'][:, :], writes=['cwbr'])
        p.dma('sync', cwcr[:, :], io['cwc'][:, :], writes=['cwcr'])
        p.dma('sync', Acol[:, :], io['alog'][:, :], writes=['Acol'])
        p.dma('sync', dtbc[:, :], io['dtb'][:, :], writes=['dtbc'])
        p.op('scalar', lambda e: e.activation(out=Acol[:, 0:1], in_=Acol[:, 0:1],
             func=AF.Exp), reads=['Acol'], writes=['Acol'])
        p.op('vector', lambda e: e.tensor_scalar(out=Acol[:, 0:1], in0=Acol[:, 0:1],
             scalar1=-1.0, scalar2=None, op0=OP.mult),
             reads=['Acol'], writes=['Acol'])
        if prefix_mode == 'flag':
            p.dma('sync', flagc[:, :], io['flag'][:, :], writes=['flagc'])
        elif prefix_mode == 'input':
            p.dma('sync', H[:, :], io['hinit'][:, :],
                  reads=[f'H_{i}' for i in range(NPAIR)],
                  writes=[f'H_{i}' for i in range(NPAIR)])

        def load_x(dram_bf, tag):
            for i in range(16):
                p.dma('sync', XT[i][:, 0:M + 3], dram_bf[i * 128:(i + 1) * 128, :],
                      reads=[f'{tag}_{i}', f'XT{i}'], writes=[f'XT{i}'])

        dtsc = psF[0:32, 0:512]
        dtsc2 = dtsc2s
        dtg = psG[0:32, 0:512]

        def dt_path():
            p.dma('sync', wpb[:, 0:NDT * 32], io['wdt'][:, :],
                  reads=['wpb'], writes=['wpb'])
            p.op('vector', lambda e: e.tensor_copy(wpbq[:, 0:NDT * 32],
                 wpb[:, 0:NDT * 32]), reads=['wpb', 'wpbq'], writes=['wpbq'])
            for tc in range(MT):
                for c in range(NDT):
                    p.op('tensor', (lambda cc, tt: lambda e: e.matmul(
                        psE[0:32, 0:512], wpbq[:, cc * 32:(cc + 1) * 32],
                        XT[cc][:, 3 + tt * 512: 3 + tt * 512 + 512],
                        start=(cc == 0), stop=(cc == NDT - 1)))(c, tc),
                        reads=['wpbq', f'XT{c}'], writes=['psE'])
                p.op('vector', lambda e: e.tensor_scalar(
                    out=dtsc[:, :], in0=psE[0:32, 0:512], scalar1=dtbc[0:32, 0:1],
                    scalar2=None, op0=OP.add), reads=['psE', 'dtbc'], writes=['psF'])
                p.op('vector', lambda e: e.tensor_scalar(
                    out=dtsc2[:, :], in0=dtsc[:, :], scalar1=0.0, scalar2=2.0,
                    op0=OP.max, op1=OP.mult), reads=['psF', 'dtsc2'],
                    writes=['dtsc2'])
                p.op('vector', lambda e: e.scalar_tensor_tensor(
                    out=dtg[:, :], in0=dtsc2[:, :], scalar=1.0, in1=dtsc[:, :],
                    op0=OP.mult, op1=OP.subtract), reads=['dtsc2', 'psF'],
                    writes=['psG'])
                p.op('scalar', lambda e: e.activation(out=dtsc[:, :], in_=dtg[:, :],
                     func=AF.Exp, scale=-1.0), reads=['psG'], writes=['psF'])
                p.op('scalar', lambda e: e.activation(out=dtg[:, :], in_=dtsc[:, :],
                     func=AF.Ln, bias=ones_col_f[0:32, 0:1]),
                     reads=['psF', 'onescf'], writes=['psG'])
                p.op('vector', (lambda tt: lambda e: e.scalar_tensor_tensor(
                    out=dtLE[0:32, tt * 512:(tt + 1) * 512], in0=dtsc2[:, :],
                    scalar=0.5, in1=dtg[:, :], op0=OP.mult, op1=OP.add))(tc),
                    reads=['dtsc2', 'psG'], writes=['dtLE_dt'])

        def chunk_pre():
            for c in range(NCH):
                p.op('vector', (lambda cc: lambda e: e.tensor_scalar(
                    out=dtsc[:, 0:128], in0=dtLE[0:32, cc * 128:(cc + 1) * 128],
                    scalar1=Acol[0:32, 0:1], scalar2=None, op0=OP.mult))(c),
                    reads=['dtLE_dt', 'Acol'], writes=['psF'])
                p.op('vector', (lambda cc: lambda e: e.tensor_tensor_scan(
                    out=dtLE[32:64, cc * 128:(cc + 1) * 128], data0=dtsc[:, 0:128],
                    data1=zeros32[:, :], initial=0.0, op0=OP.add, op1=OP.add))(c),
                    reads=['psF', 'z32'], writes=['dtLE_L'])
                p.op('vector', (lambda cc: lambda e: e.tensor_scalar(
                    out=dtsc[:, 0:128], in0=dtLE[32:64, cc * 128:(cc + 1) * 128],
                    scalar1=dtLE[32:64, cc * 128 + 127:cc * 128 + 128],
                    scalar2=None, op0=OP.subtract))(c),
                    reads=['dtLE_L'], writes=['psF'])
                p.op('scalar', (lambda cc: lambda e: e.activation(
                    out=dtLE[64:96, cc * 128:(cc + 1) * 128], in_=dtsc[:, 0:128],
                    func=AF.Exp, scale=-1.0))(c), reads=['psF'], writes=['dtLE_ef'])
                p.op('tensor', (lambda cc: lambda e: e.transpose(
                    psG[:, 0:96], dtLE[0:96, cc * 128:(cc + 1) * 128],
                    ident_f[0:96, 0:96]))(c),
                    reads=['dtLE_dt', 'dtLE_L', 'dtLE_ef', 'identf'],
                    writes=['psG'])
                p.op('vector', (lambda cc: lambda e: e.tensor_copy(
                    dtlm[:, cc * 96:(cc + 1) * 96], psG[:, 0:96]))(c),
                    reads=['psG'], writes=[f'dtlm_{c}'])

        def proj_pair(i, tiles, halo):
            for t in tiles:
                p.dma('sync', wpb[:, 0:2048], io['winp'][i, t, :, :],
                      reads=['wpb'], writes=['wpb'])
                p.op('vector', lambda e: e.tensor_copy(wpbq[:, :], wpb[:, 0:2048]),
                     reads=['wpb', 'wpbq'], writes=['wpbq'])
                tcs = list(range(MT)) + ([-1] if (halo and t != 1) else [])
                for tc in tcs:
                    if tc >= 0:
                        lo, w = 3 + tc * 512, 512
                    else:
                        lo, w = 0, 3
                    pbank, pkey = (psA, 'psA') if (tc % 2 == 0) else (psB, 'psB')
                    for c in range(NDT):
                        p.op('tensor', (lambda cc, lo_, w_, pb: lambda e: e.matmul(
                            pb[:, 0:w_], wpbq[:, cc * 128:(cc + 1) * 128],
                            XT[cc][:, lo_:lo_ + w_],
                            start=(cc == 0), stop=(cc == NDT - 1)))(c, lo, w, pbank),
                            reads=['wpbq', f'XT{c}'], writes=[pkey])
                    if t == 0:
                        dst, dk, func = xsb, 'xsb', AF.Copy
                    elif t == 1:
                        dst, dk, func = zsil, 'zsil', AF.Silu
                    elif t in (2, 3):
                        dst, dk, func = Bb[t - 2], f'Bb{t - 2}', AF.Copy
                    else:
                        dst, dk, func = Cb[t - 4], f'Cb{t - 4}', AF.Copy
                    if t == 1:
                        dsl = slice(lo - 3, lo - 3 + w)
                        wkeys = [f'{dk}_t{tc}']
                    elif tc < 0:
                        dsl = slice(0, 3)
                        wkeys = [f'{dk}_h']
                    else:
                        dsl = slice(lo, lo + w)
                        wkeys = [f'{dk}_t{tc}']
                    p.op('scalar', (lambda d, s_, pb, fn, w_: lambda e: e.activation(
                        out=d[:, s_], in_=pb[:, 0:w_], func=fn))(dst, dsl, pbank,
                                                                 func, w),
                        reads=[pkey], writes=wkeys)
                    p.pump(1)

        def conv_pair(i, srcs):
            banks = [(psA, 'psA'), (psB, 'psB'), (psC, 'psC'), (psD, 'psD')]
            for (src, skey, wtile, wtkey, wcol) in srcs:
                for k in range(KC):
                    p.op('vector', (lambda kk, wt, wc: lambda e: e.tensor_scalar(
                        out=diag[kk][:, :], in0=ident_bf[:, :],
                        scalar1=wt[:, wc + kk:wc + kk + 1], scalar2=None,
                        op0=OP.mult))(k, wtile, wcol),
                        reads=['identb', wtkey], writes=[f'diag{k}'])
                for tc in range(MT):
                    pb, pk = banks[tc % 4]
                    rd = [f'{skey}_t{tc}']
                    rd.append(f'{skey}_h' if tc == 0 else f'{skey}_t{tc - 1}')
                    for k in range(KC):
                        p.op('tensor', (lambda kk, tt, pbb, s2: lambda e: e.matmul(
                            pbb[:, 0:512], diag[kk][:, :],
                            s2[:, tt * 512 + kk:tt * 512 + kk + 512],
                            start=(kk == 0), stop=(kk == KC - 1)))(k, tc, pb, src),
                            reads=[f'diag{k}'] + rd, writes=[pk])
                for tc in range(MT):
                    pb, pk = banks[tc % 4]
                    p.op('scalar', (lambda tt, pbb, s2: lambda e: e.activation(
                        out=s2[:, 3 + tt * 512:3 + tt * 512 + 512],
                        in_=pbb[:, 0:512], func=AF.Silu))(tc, pb, src),
                        reads=[pk], writes=[f'{skey}_t{tc}'])
                    p.pump(1)

        def ssd_pair(i, full):
            h0 = 2 * i
            for c in range(NCH):
                p.pump(1)
                tb = (c * 128) // 512
                cp = c % 2
                p.op('scalar', (lambda cc: lambda e: e.activation(
                    out=expLs[:, :], in_=dtlm[:, cc * 96 + 32:cc * 96 + 64],
                    func=AF.Exp))(c), reads=[f'dtlm_{c}', 'expLs'], writes=['expLs'])
                p.op('vector', (lambda cc: lambda e: e.tensor_tensor(
                    out=alphab[:, :], in0=dtlm[:, cc * 96 + 64:cc * 96 + 96],
                    in1=expLs[:, :], op=OP.mult))(c),
                    reads=[f'dtlm_{c}', 'expLs', 'alphab'], writes=['alphab'])
                p.op('tensor', (lambda cc, kp: lambda e: e.transpose(
                    psH[:, 256 * kp:256 * kp + 128],
                    xsb[:, 3 + cc * 128:3 + cc * 128 + 128],
                    ident_bf[:, :]))(c, cp),
                    reads=[f'xsb_t{tb}', 'identb'], writes=['psH'])
                p.op('vector', (lambda kp: lambda e: e.tensor_copy(
                    xctm[kp][:, :], psH[:, 256 * kp:256 * kp + 128]))(cp),
                    reads=['psH', f'xctm{cp}'], writes=[f'xctm{cp}'])
                if full:
                    p.op('vector', (lambda ii, kp: lambda e: e.tensor_copy(
                        Hb2[kp][:, :], H[:, ii * 128:(ii + 1) * 128]))(i, cp),
                        reads=[f'H_{i}', f'Hb{cp}'], writes=[f'Hb{cp}'])
                for hl in range(2):
                    h = h0 + hl
                    bkey = f'Bb{hl}_t{tb}'
                    ckey = f'Cb{hl}_t{tb}'
                    spg, spgk = (psG, 'psG') if hl == 0 else (psD, 'psD')
                    p.op('tensor', (lambda cc, hh: lambda e: e.transpose(
                        psH[:, 128 + 256 * hh:256 + 256 * hh],
                        Bb[hh][:, 3 + cc * 128:3 + cc * 128 + 128],
                        ident_bf[:, :]))(c, hl),
                        reads=[bkey, 'identb'], writes=['psH'])
                    p.op('vector', (lambda hh: lambda e: e.tensor_copy(
                        btm[hh][:, :], psH[:, 128 + 256 * hh:256 + 256 * hh]))(hl),
                        reads=['psH', f'btm{hl}'], writes=[f'btm{hl}'])
                    if full:
                        p.op('vector', (lambda cc, hh, hg, kp: lambda e: e.tensor_scalar(
                            out=dtx1[hh][:, :], in0=xctm[kp][:, hh * 64:(hh + 1) * 64],
                            scalar1=dtlm[:, cc * 96 + hg:cc * 96 + hg + 1],
                            scalar2=None, op0=OP.mult))(c, hl, h, cp),
                            reads=[f'xctm{cp}', f'dtlm_{c}', f'dtx1_{hl}'],
                            writes=[f'dtx1_{hl}'])
                    p.op('vector', (lambda cc, hh, hg, kp: lambda e: e.tensor_scalar(
                        out=dtx2[hh][:, :], in0=xctm[kp][:, hh * 64:(hh + 1) * 64],
                        scalar1=dtlm[:, cc * 96 + hg:cc * 96 + hg + 1],
                        scalar2=dtlm[:, cc * 96 + 64 + hg:cc * 96 + 64 + hg + 1],
                        op0=OP.mult, op1=OP.mult))(c, hl, h, cp),
                        reads=[f'xctm{cp}', f'dtlm_{c}', f'dtx2_{hl}'],
                        writes=[f'dtx2_{hl}'])
                    if full:
                        p.op('tensor', (lambda cc, hg, pg: lambda e: e.transpose(
                            pg[0:1, 0:128],
                            dtlm[:, cc * 96 + 32 + hg:cc * 96 + 32 + hg + 1],
                            ident_f[:, :]))(c, h, spg),
                            reads=[f'dtlm_{c}', 'identf'], writes=[spgk])
                        p.op('vector', (lambda hh, pg: lambda e: e.tensor_copy(
                            Lrow[hh][0:1, :], pg[0:1, 0:128]))(hl, spg),
                            reads=[spgk, 'Lrow'], writes=['Lrow'])
                        p.op('scalar', (lambda hh: lambda e: e.activation(
                            out=eLrow[hh][0:1, :], in_=Lrow[hh][0:1, :],
                            func=AF.Exp))(hl),
                            reads=['Lrow', 'eLrow'], writes=['eLrow'])
                        p.op('tensor', (lambda hh: lambda e: e.matmul(
                            psE[:, 0:128], ones_row_f[0:1, :], Lrow[hh][0:1, :],
                            start=True, stop=True))(hl),
                            reads=['Lrow', 'onesrf'], writes=['psE'])
                        p.op('tensor', (lambda hh: lambda e: e.matmul(
                            psF[:, 0:128], ones_row_f[0:1, :], eLrow[hh][0:1, :],
                            start=True, stop=True))(hl),
                            reads=['eLrow', 'onesrf'], writes=['psF'])
                        p.op('vector', (lambda hh: lambda e: e.tensor_copy(
                            eLtb[hh][:, :], psF[:, 0:128]))(hl),
                            reads=['psF', 'eLtb'], writes=['eLtb'])
                        p.op('tensor', (lambda cc, hh, pg: lambda e: e.matmul(
                            pg[:, 0:128], Bb[hh][:, 3 + cc * 128:3 + cc * 128 + 128],
                            Cb[hh][:, 3 + cc * 128:3 + cc * 128 + 128],
                            start=True, stop=True))(c, hl, spg),
                            reads=[bkey, ckey, 'Lrow'], writes=[spgk])
                        p.op('vector', (lambda cc, hg, hh: lambda e: e.scalar_tensor_tensor(
                            out=Dp[hh][:, :], in0=psE[:, 0:128],
                            scalar=dtlm[:, cc * 96 + 32 + hg:cc * 96 + 32 + hg + 1],
                            in1=maskT[:, :], op0=OP.subtract, op1=OP.add))(c, h, hl),
                            reads=['psE', f'dtlm_{c}', 'maskT', 'Dp'],
                            writes=['Dp'])
                        p.op('scalar', (lambda hh: lambda e: e.activation(
                            out=Mp[hh][:, :], in_=Dp[hh][:, :], func=AF.Exp))(hl),
                            reads=['Dp', 'Mp'], writes=['Mp'])
                        p.op('vector', (lambda hh, pg: lambda e: e.tensor_tensor(
                            out=Gt[hh][:, :], in0=pg[:, 0:128], in1=Mp[hh][:, :],
                            op=OP.mult))(hl, spg),
                            reads=[spgk, 'Mp', f'Gt{hl}'], writes=[f'Gt{hl}'])
                        p.op('vector', (lambda cc, hh: lambda e: e.tensor_tensor(
                            out=Cb[hh][:, 3 + cc * 128:3 + cc * 128 + 128],
                            in0=Cb[hh][:, 3 + cc * 128:3 + cc * 128 + 128],
                            in1=eLtb[hh][:, :], op=OP.mult))(c, hl),
                            reads=[ckey, 'eLtb'], writes=[ckey])
                        ypb, ypk = (psA, 'psA') if hl == 0 else (psC, 'psC')
                        p.op('tensor', (lambda hh, pb2_: lambda e: e.matmul(
                            pb2_[0:64, 0:128], dtx1[hh][:, :], Gt[hh][:, :],
                            start=True, stop=False))(hl, ypb),
                            reads=[f'dtx1_{hl}', f'Gt{hl}'], writes=[ypk])
                        p.op('tensor', (lambda cc, hh, pb2_, kp: lambda e: e.matmul(
                            pb2_[0:64, 0:128],
                            Hb2[kp][:, hh * 64:(hh + 1) * 64],
                            Cb[hh][:, 3 + cc * 128:3 + cc * 128 + 128],
                            start=False, stop=True))(c, hl, ypb, cp),
                            reads=[f'Hb{cp}', ckey], writes=[ypk])
                    p.op('tensor', (lambda hh: lambda e: e.matmul(
                        psB[:, hh * 64:(hh + 1) * 64], btm[hh][:, :],
                        dtx2[hh][:, :], start=True, stop=True))(hl),
                        reads=[f'btm{hl}', f'dtx2_{hl}'], writes=['psB'])
                p.op('vector', (lambda ii, hg0: lambda e: e.tensor_tensor(
                    out=H[:, ii * 128:(ii + 1) * 128].rearrange(
                        "p (a b) -> p a b", a=2),
                    in0=H[:, ii * 128:(ii + 1) * 128].rearrange(
                        "p (a b) -> p a b", a=2),
                    in1=alphab[:, hg0:hg0 + 2].unsqueeze(2).broadcast_to(
                        [128, 2, 64]),
                    op=OP.mult))(i, h0),
                    reads=[f'H_{i}', 'alphab'], writes=[f'H_{i}'])
                p.op('vector', (lambda ii: lambda e: e.tensor_tensor(
                    out=H[:, ii * 128:(ii + 1) * 128],
                    in0=H[:, ii * 128:(ii + 1) * 128],
                    in1=psB[:, 0:128], op=OP.add))(i),
                    reads=[f'H_{i}', 'psB'], writes=[f'H_{i}'])
                if full:
                    p.op('scalar', (lambda cc, ii: lambda e: e.activation(
                        out=Yt[ii][0:64, cc * 128:(cc + 1) * 128],
                        in_=psA[0:64, 0:128],
                        func=AF.Copy))(c, i), reads=['psA'], writes=[f'Yt{i}'])
                    p.op('scalar', (lambda cc, ii: lambda e: e.activation(
                        out=Yt[ii][64:128, cc * 128:(cc + 1) * 128],
                        in_=psC[0:64, 0:128],
                        func=AF.Copy))(c, i), reads=['psC'], writes=[f'Yt{i}'])

        def rms_gate_pair(i):
            sqscr = wpbq
            rmsr = dtsc2s[0:2, 0:512]
            p.op('scalar', (lambda ii: lambda e: e.activation(
                out=sqscr[:, :], in_=Yt[ii][:, :], func=AF.Square))(i),
                reads=[f'Yt{i}', 'wpbq'], writes=['wpbq'])
            for j in range(MT):
                p.op('tensor', (lambda jj: lambda e: e.matmul(
                    psE[0:2, 0:512], blk2[:, :], sqscr[:, jj * 512:(jj + 1) * 512],
                    start=True, stop=True))(j),
                    reads=['wpbq', 'blk2'], writes=['psE'])
                p.op('vector', lambda e: e.tensor_copy(rmsr[:, :], psE[0:2, 0:512]),
                     reads=['psE', 'dtsc2'], writes=['dtsc2'])
                p.op('scalar', lambda e: e.activation(out=rmsr[:, :], in_=rmsr[:, :],
                     func=AF.Sqrt, scale=1.0 / 64, bias=eps6[0:2, 0:1]),
                     reads=['dtsc2', 'eps6'], writes=['dtsc2'])
                p.op('vector', lambda e: e.reciprocal(rmsr[:, :], rmsr[:, :]),
                     reads=['dtsc2'], writes=['dtsc2'])
                p.op('tensor', lambda e: e.matmul(
                    psE[:, 0:512], bcast2f[:, :], rmsr[:, :],
                    start=True, stop=True),
                    reads=['dtsc2', 'bc2'], writes=['psE'])
                p.op('vector', (lambda jj, ii: lambda e: e.tensor_tensor(
                    out=tg[:, :], in0=Yt[ii][:, jj * 512:(jj + 1) * 512],
                    in1=psE[:, 0:512], op=OP.mult))(j, i),
                    reads=[f'Yt{i}', 'psE'], writes=['tg'])
                p.op('vector', (lambda jj, ii: lambda e: e.tensor_tensor(
                    out=Yt[ii][:, jj * 512:(jj + 1) * 512], in0=tg[:, :],
                    in1=zsil[:, jj * 512:(jj + 1) * 512], op=OP.mult))(j, i),
                    reads=['tg', 'zsil'], writes=[f'Yt{i}'])

        if prefix_mode == 'flag':
            load_x(xp_d, 'xpd')
            dt_path()
            chunk_pre()
            p.op(g, lambda e: e.memset(xsb[:, 0:3], 0.0), writes=['xsb_h'])
            p.op(g, lambda e: e.memset(Bb[0][:, 0:3], 0.0), writes=['Bb0_h'])
            p.op(g, lambda e: e.memset(Bb[1][:, 0:3], 0.0), writes=['Bb1_h'])
            for i in range(NPAIR):
                proj_pair(i, [0, 2, 3], halo=False)
                conv_pair(i, [(xsb, 'xsb', cwxr, 'cwxr', i * 4),
                              (Bb[0], 'Bb0', cwbr, 'cwbr', (2 * i) * 4),
                              (Bb[1], 'Bb1', cwbr, 'cwbr', (2 * i + 1) * 4)])
                ssd_pair(i, full=False)
            for i in range(NPAIR):
                p.op('vector', (lambda ii: lambda e: e.tensor_scalar(
                    out=H[:, ii * 128:(ii + 1) * 128],
                    in0=H[:, ii * 128:(ii + 1) * 128],
                    scalar1=flagc[:, 0:1], scalar2=None, op0=OP.mult))(i),
                    reads=[f'H_{i}', 'flagc'], writes=[f'H_{i}'])

        load_x(xm_d, 'xmd')
        dt_path()
        chunk_pre()
        p.op(g, lambda e: e.memset(xsb[:, 0:3], 0.0),
             writes=['xsb_h'])
        p.op(g, lambda e: e.memset(Bb[0][:, 0:3], 0.0), writes=['Bb0_h'])
        p.op(g, lambda e: e.memset(Bb[1][:, 0:3], 0.0), writes=['Bb1_h'])
        p.op(g, lambda e: e.memset(Cb[0][:, 0:3], 0.0), writes=['Cb0_h'])
        p.op(g, lambda e: e.memset(Cb[1][:, 0:3], 0.0), writes=['Cb1_h'])
        if 'dt' in debug_outputs:
            p.dma('scalar', io['dbg_dtle'][:, :], dtLE[:, :],
                  reads=['dtLE_dt', 'dtLE_L', 'dtLE_ef'], writes=['dbgdt'])
        for i in range(NPAIR):
            proj_pair(i, [0, 1, 2, 3, 4, 5], halo=True)
            if i == 0 and 'proj0' in debug_outputs:
                allt = [f'_t{t}' for t in range(MT)] + ['_h']
                p.dma('scalar', io['dbg_pxsb'][:, :], xsb[:, :],
                      reads=['xsb' + t for t in allt], writes=['dbg5'])
                p.dma('scalar', io['dbg_pbb0'][:, :], Bb[0][:, :],
                      reads=['Bb0' + t for t in allt], writes=['dbg6'])
            conv_pair(i, [(xsb, 'xsb', cwxr, 'cwxr', i * 4),
                          (Bb[0], 'Bb0', cwbr, 'cwbr', (2 * i) * 4),
                          (Bb[1], 'Bb1', cwbr, 'cwbr', (2 * i + 1) * 4),
                          (Cb[0], 'Cb0', cwcr, 'cwcr', (2 * i) * 4),
                          (Cb[1], 'Cb1', cwcr, 'cwcr', (2 * i + 1) * 4)])
            if i == 0 and 'conv0' in debug_outputs:
                allt = [f'_t{t}' for t in range(MT)] + ['_h']
                p.dma('scalar', io['dbg_xsb'][:, :], xsb[:, :],
                      reads=['xsb' + t for t in allt], writes=['dbg1'])
                p.dma('scalar', io['dbg_bb0'][:, :], Bb[0][:, :],
                      reads=['Bb0' + t for t in allt], writes=['dbg2'])
                p.dma('scalar', io['dbg_cb0'][:, :], Cb[0][:, :],
                      reads=['Cb0' + t for t in allt], writes=['dbg3'])
                p.dma('scalar', io['dbg_zsil'][:, :], zsil[:, :],
                      reads=['zsil' + f'_t{t}' for t in range(MT)], writes=['dbg4'])
            ssd_pair(i, full=True)
            if i == 0 and 'yraw' in debug_outputs:
                p.dma('scalar', io['dbg_yraw'][:, :], Yt[0][:, :],
                      reads=['Yt0'], writes=['dbg7'])
            rms_gate_pair(i)
        if 'hend' in debug_outputs:
            p.dma('scalar', io['hend'][:, :], H[:, :],
                  reads=[f'H_{j}' for j in range(NPAIR)], writes=['hendout'])
        if 'ymix' in debug_outputs:
            for i in range(16):
                p.dma('scalar', io['ymix'][i * 128:(i + 1) * 128, :], Yt[i][:, :],
                      reads=[f'Yt{i}'], writes=['ymixout'])

        p.drain()
        p.fence()
        m_stack.close()
        w_stack = ExitStack()
        wsb = mksb(w_stack)
        wq = [wsb("wq0", [128, 2048], BF16), wsb("wq1", [128, 2048], BF16)]
        xres = [wsb("xres0", [128, 512], F32), wsb("xres1", [128, 512], F32)]
        pre = [wsb(f"pre{i}", [128, 512], F32) for i in range(16)]
        x2sb = wsb("x2sb", [128, 512], BF16)
        sqbf = wsb("sqbf", [128, 512], BF16)
        prebf = wsb("prebf", [128, 512], BF16)
        rowm = wsb("rowm", [1, 512], F32)
        rowv = wsb("rowv", [1, 512], F32)
        rowr = wsb("rowr", [1, 512], F32)
        rowmr = wsb("rowmr", [1, 512], F32)

        def ln_apply(tc, pret, prek, psq, psqk, psm, psmk):
            for dm in range(16):
                p.op('scalar', (lambda d: lambda e: e.activation(
                    out=sqbf[:, :], in_=pret[d][:, :], func=AF.Square))(dm),
                    reads=[f'{prek}{dm}'], writes=['sqbf'])
                p.op('tensor', (lambda d: lambda e: e.matmul(
                    psq[0:1, 0:512], ones_col_b[:, 0:1], sqbf[:, :],
                    start=(d == 0), stop=(d == 15)))(dm),
                    reads=['sqbf', 'onescb'], writes=[psqk])
                p.op('scalar', (lambda d: lambda e: e.activation(
                    out=prebf[:, :], in_=pret[d][:, :], func=AF.Copy))(dm),
                    reads=[f'{prek}{dm}'], writes=['prebf'])
                p.op('tensor', (lambda d: lambda e: e.matmul(
                    psm[0:1, 0:512], ones_col_b[:, 0:1], prebf[:, :],
                    start=(d == 0), stop=(d == 15)))(dm),
                    reads=['prebf', 'onescb'], writes=[psmk])
            p.op('vector', lambda e: e.tensor_scalar(
                out=rowm[0:1, :], in0=psm[0:1, 0:512], scalar1=1.0 / D,
                scalar2=None, op0=OP.mult), reads=[psmk], writes=['rowm'])
            p.op('vector', lambda e: e.tensor_scalar(
                out=rowv[0:1, :], in0=psq[0:1, 0:512], scalar1=1.0 / D,
                scalar2=None, op0=OP.mult), reads=[psqk], writes=['rowv'])
            p.op('vector', lambda e: e.tensor_tensor(
                out=rowr[0:1, :], in0=rowm[0:1, :], in1=rowm[0:1, :], op=OP.mult),
                reads=['rowm'], writes=['rowr'])
            p.op('vector', lambda e: e.tensor_tensor(
                out=rowv[0:1, :], in0=rowv[0:1, :], in1=rowr[0:1, :],
                op=OP.subtract), reads=['rowv', 'rowr'], writes=['rowv'])
            p.op('scalar', lambda e: e.activation(out=rowv[0:1, :],
                 in_=rowv[0:1, :], func=AF.Sqrt, bias=eps5[0:1, 0:1]),
                 reads=['rowv', 'eps5'], writes=['rowv'])
            p.op('vector', lambda e: e.reciprocal(rowr[0:1, :], rowv[0:1, :]),
                 reads=['rowv'], writes=['rowr'])
            p.op('vector', lambda e: e.tensor_tensor(
                out=rowmr[0:1, :], in0=rowm[0:1, :], in1=rowr[0:1, :], op=OP.mult),
                reads=['rowm', 'rowr'], writes=['rowmr'])
            p.op('tensor', lambda e: e.matmul(psq[:, 0:512], ones_row_f[0:1, :],
                 rowr[0:1, :], start=True, stop=True),
                 reads=['rowr', 'onesrf'], writes=[psqk])
            p.op('tensor', lambda e: e.matmul(psm[:, 0:512], ones_row_f[0:1, :],
                 rowmr[0:1, :], start=True, stop=True),
                 reads=['rowmr', 'onesrf'], writes=[psmk])
            for dm in range(16):
                p.op('vector', (lambda d: lambda e: e.tensor_tensor(
                    out=pret[d][:, :], in0=pret[d][:, :], in1=psq[:, 0:512],
                    op=OP.mult))(dm),
                    reads=[f'{prek}{dm}', psqk], writes=[f'{prek}{dm}'])
                p.op('vector', (lambda d: lambda e: e.tensor_tensor(
                    out=x2sb[:, :], in0=pret[d][:, :],
                    in1=psm[:, 0:512], op=OP.subtract))(dm),
                    reads=[f'{prek}{dm}', psmk, 'x2sb'], writes=['x2sb'])
                p.dma('scalar', x2_d[dm, :, tc * 512:(tc + 1) * 512], x2sb[:, :],
                      reads=['x2sb'], writes=[f'x2d_{dm}'])

        for tc in range(MT):
            for dm in range(16):
                b = dm % 2
                p.dma('sync', wq[b][:, :], woq_d[dm, :, :],
                      reads=[f'woqd_{dm}', f'wq{b}'], writes=[f'wq{b}'])
                p.dma('scalar', xres[b][:, :],
                      io['xmain'][dm * 128:(dm + 1) * 128,
                                  3 + tc * 512:3 + tc * 512 + 512],
                      reads=[f'xres{b}'], writes=[f'xres{b}'])
                pb, pk = (psA, 'psA') if dm % 2 == 0 else (psB, 'psB')
                for c in range(16):
                    p.op('tensor', (lambda cc, bb, pbb, t: lambda e: e.matmul(
                        pbb[:, 0:512], wq[bb][:, cc * 128:(cc + 1) * 128],
                        Yt[cc][:, t * 512:(t + 1) * 512],
                        start=(cc == 0), stop=(cc == 15)))(c, b, pb, tc),
                        reads=[f'wq{b}', f'Yt{c}'], writes=[pk])
                p.op('vector', (lambda d, bb, pbb, pkk: lambda e: e.tensor_tensor(
                    out=pre[d][:, :], in0=pbb[:, 0:512], in1=xres[bb][:, :],
                    op=OP.add))(dm, b, pb, pk),
                    reads=[pk, f'xres{b}'], writes=[f'pre{dm}'])
            ln_apply(tc, pre, 'pre', psC, 'psC', psD, 'psD')
        w_stack.close()
        y_stack.close()
    else:
        x_stack = ExitStack()
        x2ldb = x_stack.enter_context(nc.sbuf_tensor("x2ldb", [128, M], F32))
        x2cst = x_stack.enter_context(nc.sbuf_tensor("x2cst", [128, M], BF16))
        for i in range(16):
            p.dma('sync', x2ldb[:, :], io['x2t'][i * 128:(i + 1) * 128, :],
                  reads=['x2ldb'], writes=['x2ldb'])
            p.op('vector', lambda e: e.tensor_copy(x2cst[:, :], x2ldb[:, :]),
                 reads=['x2ldb', 'x2cst'], writes=['x2cst'])
            p.dma('scalar', x2_d[i, :, :], x2cst[:, :],
                  reads=['x2cst'], writes=[f'x2d_{i}'])
        x_stack.close()
        m_stack.close()
        y_stack.close()

    if do_ffn:
        p.drain()
        p.fence()
        f_stack = ExitStack()
        fsb = mksb(f_stack)
        x2bf = [fsb(f"x2bf{i}", [128, M], BF16) for i in range(16)]
        b1cs = fsb("b1cs", [128, 64], F32)
        b2cs = fsb("b2cs", [128, 16], F32)
        for i in range(16):
            p.dma('sync', x2bf[i][:, :], x2_d[i, :, :],
                  reads=[f'x2d_{i}'], writes=[f'x2bf{i}'])
        f1_stack = ExitStack()
        f1sb = mksb(f1_stack)
        w1b = [f1sb("w1b0", [128, 2048], F32), f1sb("w1b1", [128, 2048], F32)]
        w1mn = f1sb("w1mn", [128, 2048], BF16)
        w1q = f1sb("w1q", [128, 2048], BF16)
        hsb = [f1sb("hsb0", [128, 512], BF16), f1sb("hsb1", [128, 512], BF16)]
        p.dma('sync', b1cs[:, :], io['b1c'][:, :], writes=['b1cs'])
        p.dma('sync', b2cs[:, :], io['b2c'][:, :], writes=['b2cs'])
        for ft in range(64):
            b = ft % 2
            p.dma('sync', w1b[b][:, :], io['w1t'][ft, :, :],
                  reads=[f'w1b{b}'], writes=[f'w1b{b}'])
            p.op('vector', (lambda bb: lambda e: e.tensor_scalar(
                out=w1mn[:, :], in0=w1b[bb][:, :], scalar1=thr['n1'][:, 0:1],
                scalar2=None, op0=OP.is_le))(b),
                reads=[f'w1b{b}', 'thr_n1'], writes=['w1mn'])
            p.op('vector', (lambda bb: lambda e: e.scalar_tensor_tensor(
                out=w1q[:, :], in0=w1b[bb][:, :], scalar=thr['t1'][:, 0:1],
                in1=w1mn[:, :], op0=OP.is_ge, op1=OP.subtract))(b),
                reads=[f'w1b{b}', 'thr_t1', 'w1mn'], writes=['w1q'])
            for tc in range(MT):
                pb, pk = (psA, 'psA') if tc % 2 == 0 else (psB, 'psB')
                for c in range(NDT):
                    p.op('tensor', (lambda cc, t, pbb: lambda e: e.matmul(
                        pbb[:, 0:512], w1q[:, cc * 128:(cc + 1) * 128],
                        x2bf[cc][:, t * 512:(t + 1) * 512],
                        start=(cc == 0), stop=(cc == NDT - 1)))(c, tc, pb),
                        reads=['w1q', f'x2bf{c}'], writes=[pk])
                hb = tc % 2
                p.op('scalar', (lambda f_, pbb, bb: lambda e: e.activation(
                    out=hsb[bb][:, :], in_=pbb[:, 0:512], func=AF.Gelu,
                    bias=b1cs[:, f_:f_ + 1]))(ft, pb, hb),
                    reads=[pk, 'b1cs', f'hsb{hb}'], writes=[f'hsb{hb}'])
                p.dma('scalar', hscr_d[:, tc, ft, :], hsb[hb][:, :],
                      reads=[f'hsb{hb}'], writes=[f'hscr_{tc}_{ft}'])
        p.fence()
        f1_stack.close()
        f2_stack = ExitStack()
        f2sb = mksb(f2_stack)
        hslab = f2sb("hslab", [128, 64 * 512], BF16)
        w2qb = [f2sb("w2qb0", [128, 4096], BF16), f2sb("w2qb1", [128, 4096], BF16)]
        pre2 = [f2sb(f"pre2_{i}", [128, 512], F32) for i in range(16)]
        out_sb = f2sb("out_sb", [128, 512], F32)
        sq2 = f2sb("sq2", [128, 512], BF16)
        pb2 = f2sb("pb2", [128, 512], BF16)
        r2m = f2sb("r2m", [1, 512], F32)
        r2v = f2sb("r2v", [1, 512], F32)
        r2r = f2sb("r2r", [1, 512], F32)
        r2mr = f2sb("r2mr", [1, 512], F32)

        def ln2_apply(tc):
            for dm in range(16):
                p.op('scalar', (lambda d: lambda e: e.activation(
                    out=sq2[:, :], in_=pre2[d][:, :], func=AF.Square))(dm),
                    reads=[f'pre2_{dm}'], writes=['sq2'])
                p.op('tensor', (lambda d: lambda e: e.matmul(
                    psC[0:1, 0:512], ones_col_b[:, 0:1], sq2[:, :],
                    start=(d == 0), stop=(d == 15)))(dm),
                    reads=['sq2', 'onescb'], writes=['psC'])
                p.op('scalar', (lambda d: lambda e: e.activation(
                    out=pb2[:, :], in_=pre2[d][:, :], func=AF.Copy))(dm),
                    reads=[f'pre2_{dm}'], writes=['pb2'])
                p.op('tensor', (lambda d: lambda e: e.matmul(
                    psD[0:1, 0:512], ones_col_b[:, 0:1], pb2[:, :],
                    start=(d == 0), stop=(d == 15)))(dm),
                    reads=['pb2', 'onescb'], writes=['psD'])
            p.op('vector', lambda e: e.tensor_scalar(
                out=r2m[0:1, :], in0=psD[0:1, 0:512], scalar1=1.0 / D,
                scalar2=None, op0=OP.mult), reads=['psD'], writes=['r2m'])
            p.op('vector', lambda e: e.tensor_scalar(
                out=r2v[0:1, :], in0=psC[0:1, 0:512], scalar1=1.0 / D,
                scalar2=None, op0=OP.mult), reads=['psC'], writes=['r2v'])
            p.op('vector', lambda e: e.tensor_tensor(
                out=r2r[0:1, :], in0=r2m[0:1, :], in1=r2m[0:1, :], op=OP.mult),
                reads=['r2m'], writes=['r2r'])
            p.op('vector', lambda e: e.tensor_tensor(
                out=r2v[0:1, :], in0=r2v[0:1, :], in1=r2r[0:1, :],
                op=OP.subtract), reads=['r2v', 'r2r'], writes=['r2v'])
            p.op('scalar', lambda e: e.activation(out=r2v[0:1, :], in_=r2v[0:1, :],
                 func=AF.Sqrt, bias=eps5[0:1, 0:1]),
                 reads=['r2v', 'eps5'], writes=['r2v'])
            p.op('vector', lambda e: e.reciprocal(r2r[0:1, :], r2v[0:1, :]),
                 reads=['r2v'], writes=['r2r'])
            p.op('vector', lambda e: e.tensor_tensor(
                out=r2mr[0:1, :], in0=r2m[0:1, :], in1=r2r[0:1, :], op=OP.mult),
                reads=['r2m', 'r2r'], writes=['r2mr'])
            p.op('tensor', lambda e: e.matmul(psC[:, 0:512], ones_row_f[0:1, :],
                 r2r[0:1, :], start=True, stop=True),
                 reads=['r2r', 'onesrf'], writes=['psC'])
            p.op('tensor', lambda e: e.matmul(psD[:, 0:512], ones_row_f[0:1, :],
                 r2mr[0:1, :], start=True, stop=True),
                 reads=['r2mr', 'onesrf'], writes=['psD'])
            for dm in range(16):
                p.op('vector', (lambda d: lambda e: e.tensor_tensor(
                    out=pre2[d][:, :], in0=pre2[d][:, :], in1=psC[:, 0:512],
                    op=OP.mult))(dm),
                    reads=[f'pre2_{dm}', 'psC'], writes=[f'pre2_{dm}'])
                p.op('vector', (lambda d: lambda e: e.tensor_tensor(
                    out=out_sb[:, :], in0=pre2[d][:, :], in1=psD[:, 0:512],
                    op=OP.subtract))(dm),
                    reads=[f'pre2_{dm}', 'psD', 'out_sb'], writes=['out_sb'])
                p.dma('scalar', io['out_t'][dm * 128:(dm + 1) * 128,
                      tc * 512:(tc + 1) * 512], out_sb[:, :],
                      reads=['out_sb'], writes=[f'outd_{dm}_{tc}'])

        for tc in range(MT):
            p.dma('sync', hslab[:, :],
                  hscr_d[:, tc, :, :].rearrange("p a b -> p (a b)"),
                  reads=[f'hscr_{tc}_{ft}' for ft in range(64)] + ['hslab'],
                  writes=['hslab'])
            for dm in range(16):
                pb, pk = (psA, 'psA') if dm % 2 == 0 else (psB, 'psB')
                for half in range(2):
                    hb = half
                    p.dma('sync', w2qb[hb][:, :],
                          w2q_d[dm, :, half * 4096:(half + 1) * 4096],
                          reads=[f'w2qd_{dm}', f'w2qb{hb}'], writes=[f'w2qb{hb}'])
                    for c in range(32):
                        cg = half * 32 + c
                        p.op('tensor', (lambda cc, cg_, hh, pbb: lambda e: e.matmul(
                            pbb[:, 0:512], w2qb[hh][:, cc * 128:(cc + 1) * 128],
                            hslab[:, cg_ * 512:(cg_ + 1) * 512],
                            start=(cg_ == 0), stop=(cg_ == 63)))(c, cg, hb, pb),
                            reads=[f'w2qb{hb}', 'hslab'], writes=[pk])
                p.op('vector', (lambda d, t, pbb, pkk: lambda e:
                     e.scalar_tensor_tensor(
                         out=pre2[d][:, :], in0=pbb[:, 0:512],
                         scalar=b2cs[:, d:d + 1],
                         in1=x2bf[d][:, t * 512:(t + 1) * 512].bitcast(F32)
                         if False else x2bf[d][:, t * 512:(t + 1) * 512],
                         op0=OP.add, op1=OP.add))(dm, tc, pb, pk),
                    reads=[pk, 'b2cs', f'x2bf{dm}'], writes=[f'pre2_{dm}'])
            ln2_apply(tc)
        f2_stack.close()
        f_stack.close()
    else:
        o_stack = ExitStack()
        ocst = o_stack.enter_context(nc.sbuf_tensor("ocst", [128, M], BF16))
        ocast = o_stack.enter_context(nc.sbuf_tensor("ocast", [128, M], F32))
        for i in range(16):
            p.dma('sync', ocst[:, :], x2_d[i, :, :],
                  reads=[f'x2d_{i}', 'ocst'], writes=['ocst'])
            p.op('vector', lambda e: e.tensor_copy(ocast[:, :], ocst[:, :]),
                 reads=['ocst', 'ocast'], writes=['ocast'])
            p.dma('scalar', io['out_t'][i * 128:(i + 1) * 128, :], ocast[:, :],
                  reads=['ocast'], writes=['outd'])
        o_stack.close()

    p.finish()
    p.emit()
    g_stack.close()
    return nc


def layout_weights(ins):
    Win = np.ascontiguousarray(np.asarray(ins['L0_attn_Win']), dtype=np.float32)
    Wout = np.ascontiguousarray(np.asarray(ins['L0_attn_Wout']), dtype=np.float32)
    w1 = np.ascontiguousarray(np.asarray(ins['L0_ffn_w1']), dtype=np.float32)
    w2 = np.ascontiguousarray(np.asarray(ins['L0_ffn_w2']), dtype=np.float32)
    Win_T = np.ascontiguousarray(Win.T)
    HPo = NH * P
    winp = np.empty((NPAIR, 6, 128, NDT * 128), np.float32)
    for i in range(NPAIR):
        h0, h1 = 2 * i, 2 * i + 1
        cols = [
            np.r_[h0 * P + np.arange(P), h1 * P + np.arange(P)],
            np.r_[HPo + h0 * P + np.arange(P), HPo + h1 * P + np.arange(P)],
            2 * HPo + h0 * NS + np.arange(NS),
            2 * HPo + h1 * NS + np.arange(NS),
            2 * HPo + NH * NS + h0 * NS + np.arange(NS),
            2 * HPo + NH * NS + h1 * NS + np.arange(NS),
        ]
        for t, cl in enumerate(cols):
            blk = Win_T[:, cl]
            winp[i, t] = blk.reshape(NDT, 128, 128).transpose(1, 0, 2).reshape(
                128, NDT * 128)
    wdt = np.ascontiguousarray(
        Win_T[:, 2 * HPo + 2 * NH * NS:].reshape(NDT, 128, 32)
        .transpose(1, 0, 2).reshape(128, NDT * 32))
    woutt = np.ascontiguousarray(
        Wout.reshape(16, 128, 16, 128).transpose(0, 3, 2, 1).reshape(16, 128, 2048))
    w1t = np.ascontiguousarray(
        w1.reshape(64, 128, 16, 128).transpose(0, 3, 2, 1).reshape(64, 128,
                                                                   NDT * 128))
    w2t = np.ascontiguousarray(
        w2.reshape(16, 128, 64, 128).transpose(0, 3, 2, 1).reshape(16, 128,
                                                                   64 * 128))
    cwx = np.ascontiguousarray(
        np.asarray(ins['L0_attn_conv_w_x'], dtype=np.float32)
        .reshape(NPAIR, 128, KC).transpose(1, 0, 2).reshape(128, NPAIR * KC))
    cwb = np.ascontiguousarray(
        np.asarray(ins['L0_attn_conv_w_b'], dtype=np.float32)
        .transpose(1, 0, 2).reshape(128, NH * KC))
    cwc = np.ascontiguousarray(
        np.asarray(ins['L0_attn_conv_w_c'], dtype=np.float32)
        .transpose(1, 0, 2).reshape(128, NH * KC))
    b1c = np.ascontiguousarray(
        np.asarray(ins['L0_ffn_b1'], dtype=np.float32).reshape(64, 128).T)
    b2c = np.ascontiguousarray(
        np.asarray(ins['L0_ffn_b2'], dtype=np.float32).reshape(16, 128).T)
    alog = np.zeros((128, 1), np.float32)
    alog[0:32, 0] = np.asarray(ins['L0_attn_A_log'], dtype=np.float32)
    dtb = np.zeros((128, 1), np.float32)
    dtb[0:32, 0] = np.asarray(ins['L0_attn_dt_bias'], dtype=np.float32)
    return dict(winp=winp, wdt=wdt, woutt=woutt, w1t=w1t, w2t=w2t,
                cwx=cwx, cwb=cwb, cwc=cwc, b1c=b1c, b2c=b2c,
                alog=alog, dtb=dtb)


def layout_x_slice(x, b, t0, M):
    out = np.zeros((D, M + 3), np.float32)
    lo = max(0, t0 - 3)
    npad = lo - (t0 - 3)
    seg = np.ascontiguousarray(np.asarray(x[b, lo:t0 + M, :], dtype=np.float32).T)
    out[:, npad:] = seg
    return out


def kernel(**inputs):
    M = 2048
    w = layout_weights(inputs)
    x = np.asarray(inputs['L0_attn_x'])
    nc = build_nc(M=M, prefix_mode='flag')
    in_maps = []
    for core in range(NCORES):
        b, r = core // 2, core % 2
        im = dict(w)
        im['xmain'] = layout_x_slice(x, b, r * M, M)
        im['xpref'] = (layout_x_slice(x, b, 0, M) if r == 1
                       else np.zeros((D, M + 3), np.float32))
        fl = np.zeros((128, 1), np.float32)
        fl[:, 0] = float(r)
        im['flag'] = fl
        in_maps.append(im)
    res = run_bass_kernel_spmd(nc, in_maps, core_ids=list(range(NCORES)))
    out = np.empty((4, 4096, D), np.float32)
    for core in range(NCORES):
        b, r = core // 2, core % 2
        out[b, r * M:(r + 1) * M, :] = res.results[core]['out_t'].T
    return out
